# version 17
# baseline (speedup 1.0000x reference)
import contextlib
import numpy as np

import concourse.bass as bass
import concourse.mybir as mybir
from concourse.bass_utils import run_bass_kernel_spmd

F32 = mybir.dt.float32
BF16 = mybir.dt.bfloat16
I32 = mybir.dt.int32
AF = mybir.ActivationFunctionType
ALU = mybir.AluOpType
AX = mybir.AxisListType

H, DK, D, T, NB, R, NCORE = 16, 128, 2048, 1024, 4, 8192, 8
SCALE = float(DK) ** -0.5

PE, DV, AC, PO, SY = "pe", "dv", "ac", "po", "sy"
ENGS = (PE, DV, AC, PO, SY)
OWN_SEM = {PE: "pe", DV: "dv", AC: "ac", PO: "po", SY: "sy"}

_DEBUG = ()


class Buf:
    __slots__ = ("name", "writers", "readers")

    def __init__(self, name):
        self.name = name
        self.writers = {}
        self.readers = {}


class Prog:

    def __init__(self):
        self.ops = {e: [] for e in ENGS}
        self.cnt = {}
        self.waited = {e: {} for e in ENGS}
        self.fence = {}
        self.qeng = {"dg": PO, "dt": SY, "ds": AC}
        self.bounds = {"dg": {0}, "dt": {0}, "ds": {0}}
        self.tok_pos = {"dg": {0: 0}, "dt": {0: 0}, "ds": {0: 0}}
        self.pending_ins = {"dg": [], "dt": [], "ds": []}

    def ensure_boundary(self, s, c):
        if c in self.bounds[s]:
            return
        self.bounds[s].add(c)
        self.pending_ins[s].append((self.tok_pos[s][c], c))

    def finalize(self):
        for s, ins in self.pending_ins.items():
            eng = self.qeng[s]
            for idx, c in sorted(ins, key=lambda t: -t[0]):
                def emit(e, S, s=s, c=c):
                    e.wait_ge(S[s], c)
                self.ops[eng].insert(idx, emit)

    def barrier(self):
        for s, c in self.cnt.items():
            if self.fence.get(s, -1) < c:
                self.fence[s] = c
        for s in ("dg", "dt", "ds"):
            if s in self.cnt:
                self.ensure_boundary(s, self.cnt[s])

    def _collect(self, eng, writes, reads):
        need = dict(self.fence)
        for b in reads:
            for s, c in b.writers.items():
                if need.get(s, -1) < c:
                    need[s] = c
        for b in writes:
            for d in (b.writers, b.readers):
                for s, c in d.items():
                    if need.get(s, -1) < c:
                        need[s] = c
        wl = []
        for s, c in sorted(need.items()):
            if s == "pe" and eng == PE:
                continue
            if self.waited[eng].get(s, -1) < c:
                if s in self.bounds:
                    self.ensure_boundary(s, c)
                wl.append((s, c))
                self.waited[eng][s] = c
        return wl

    def op(self, eng, fns, reads=(), writes=(), inc=1, sem=None):
        if callable(fns):
            fns = [fns]
        sem = sem or OWN_SEM[eng]
        wl = self._collect(eng, writes, reads)
        n = self.cnt.get(sem, 0) + inc
        self.cnt[sem] = n
        if sem in self.tok_pos:
            self.tok_pos[sem][n] = len(self.ops[eng]) + 1

        def emit(e, S, wl=wl, fns=fns, sem=sem, inc=inc):
            for (s, c) in wl:
                e.wait_ge(S[s], c)
            ins = None
            for f in fns:
                ins = f(e)
            ins.then_inc(S[sem], inc)

        self.ops[eng].append(emit)
        for b in reads:
            if b.readers.get(sem, -1) < n:
                b.readers[sem] = n
        for b in writes:
            b.writers = {sem: n}
            b.readers = {}
        return (sem, n)

    def raw_wait(self, eng, sem, count):
        if sem in self.bounds:
            self.ensure_boundary(sem, count)

        def emit(e, S, sem=sem, count=count):
            e.wait_ge(S[sem], count)
        self.ops[eng].append(emit)


def build_nc():
    nc = bass.Bass("TRN2", target_bir_lowering=False)
    p = Prog()

    xkv = nc.dram_tensor("xkv", [NB * T, D], F32, kind="ExternalInput").ap()
    W = {}
    for n_, sh in [("wq", [D, D]), ("wk", [D, D]), ("wv", [D, D]), ("wo", [D, D]),
                   ("fq1", [DK, DK]), ("fq2", [DK, DK]), ("fk1", [DK, DK]), ("fk2", [DK, DK]),
                   ("fq1b", [DK]), ("fq2b", [DK]), ("fk1b", [DK]), ("fk2b", [DK]),
                   ("nw", [DK]), ("wg", [R, D]), ("wu", [R, D]), ("wd", [D, R])]:
        W[n_] = nc.dram_tensor(n_, sh, F32, kind="ExternalInput").ap()
    out = nc.dram_tensor("out", [T, D], F32, kind="ExternalOutput").ap()
    y_hbm = nc.dram_tensor("y_hbm", [T, D], F32).ap()
    wkT_hbm = nc.dram_tensor("wkT_hbm", [H, 128, H * 128], BF16).ap()
    wgT_hbm = nc.dram_tensor("wgT_hbm", [8, 8, 128, H * 128], BF16).ap()
    wuT_hbm = nc.dram_tensor("wuT_hbm", [8, 8, 128, H * 128], BF16).ap()
    wdT_hbm = nc.dram_tensor("wdT_hbm", [8, 4, 128, 8 * 512], BF16).ap()
    wvT_hbm = nc.dram_tensor("wvT_hbm", [4, 128, H * 512], BF16).ap()

    dbg = {}
    for name, shape in [("dbg_k2", [128, H * T]), ("dbg_v", [128, 8 * D]),
                        ("dbg_tot", [128, H * DK]), ("dbg_q2", [128, H * T]),
                        ("dbg_orms", [128, 8 * D]), ("dbg_xq", [128, H * T]),
                        ("dbg_acc", [T, D]), ("dbg_yh", [T, D]),
                        ("dbg_p0", [128, 8 * T]), ("dbg_pall", [8, 128, 8 * T]), ("dbg_wdt", [128, 8 * 512])]:
        if name in _DEBUG:
            dbg[name] = nc.dram_tensor(name, shape, F32, kind="ExternalOutput").ap()

    B = {}

    def gb(name):
        if name not in B:
            B[name] = Buf(name)
        return B[name]

    root = contextlib.ExitStack()
    with root:
        S = {s: root.enter_context(nc.semaphore(s))
             for s in ["pe", "dv", "ac", "po", "sy", "dg", "dt", "ds"]}
        PS = [root.enter_context(nc.psum_tensor(f"ps{i}", [128, 512], F32)) for i in range(8)]
        psb = [gb(f"ps{i}") for i in range(8)]

        def sb(st, name, cols, dtype, side="left"):
            return st.enter_context(nc.sbuf_tensor(name, [128, cols], dtype, side=side))

        eps5 = sb(root, "eps5", 1, F32)
        eps8 = sb(root, "eps8", 1, F32)
        bcol = {n_: sb(root, "c_" + n_, 1, F32) for n_ in ["bq1", "bq2", "bk1", "bk2", "nw"]}
        wrow = [sb(root, f"wrow{i}", 2 * D, BF16) for i in range(3)]
        srt = sb(root, "srt", 1, F32)
        rstd = sb(root, "rstd", 1, F32)
        bnst = sb(root, "bnst", 24, F32)
        mv = sb(root, "mv", 2, F32)
        bnst2 = sb(root, "bnst2", 24, F32)
        mv2 = sb(root, "mv2", 2, F32)
        stats5 = sb(root, "stats5", 8 * 4 * 6, F32)

        wrow_b = [gb(f"wrow{i}") for i in range(3)]
        sf_h = [gb(f"sf{h}") for h in range(H)]
        sbf_h = [gb(f"sbf{h % 4}") for h in range(H)]
        msT_h = [gb(f"msT{h}") for h in range(H)]
        scr = {n_: gb(n_) for n_ in ["srt", "rstd", "mv", "bnst", "stats5"]}

        block = root.enter_context(nc.Block())

        def dma_pg(outap, inap, reads=(), writes=(), accum=None):
            if accum is None:
                fn = lambda e, o=outap, i=inap: e.dma_start(out=o, in_=i)
            else:
                fn = lambda e, o=outap, i=inap, a=accum: e.dma_start(out=o, in_=i, accum_op=a)
            return p.op(PO, fn, reads=reads, writes=writes, inc=16, sem="dg")

        def dma_ds(outap, inap, reads=(), writes=()):
            return p.op(AC, lambda e, o=outap, i=inap: e.dma_start(out=o, in_=i),
                        reads=reads, writes=writes, inc=16, sem="ds")

        def dma_tr(outap, inap, reads=(), writes=()):
            return p.op(SY, lambda e, o=outap, i=inap: e.dma_start_transpose(out=o, in_=i),
                        reads=reads, writes=writes, inc=16, sem="dt")

        sSR = contextlib.ExitStack()
        maskT = sb(sSR, "maskT", 128, F32)
        iota_s = sb(sSR, "iota_s", 128, F32)
        fT = {n_: sb(sSR, "fT_" + n_, 128, BF16) for n_ in ["q1", "q2", "k1", "k2"]}
        fstage = sb(sSR, "fstage", 128, BF16)
        state_f = sb(sSR, "state_f", H * DK, F32)
        sbr = [sb(sSR, f"sbr{i}", DK, BF16) for i in range(4)]
        wTh = [sb(sSR, f"wTh{i}", H * 128, BF16) for i in range(2)]
        kqf = [sb(sSR, f"kqf{i}", T, BF16) for i in range(2)]
        t1s = [sb(sSR, f"t1s{i}", 512, F32) for i in range(2)]
        ktm = [sb(sSR, f"ktm{i}", H * 128, BF16) for i in range(2)]
        msT2 = [sb(sSR, f"msT2_{i}", H * 128, BF16) for i in range(2)]
        osc2 = [sb(sSR, "osc2_0", 128, F32)] * 2
        ssq2 = [sb(sSR, f"ssq2_{i}", 1, F32) for i in range(2)]
        srt2 = [sb(sSR, f"srt2_{i}", 1, F32) for i in range(2)]
        rst2 = [sb(sSR, f"rst2_{i}", 1, F32) for i in range(2)]
        wTh_b = [gb("wTh0"), gb("wTh1")]
        kqf_b = [gb("kqf0"), gb("kqf1")]
        t1_b = [gb("t1s0"), gb("t1s1")]
        ktm_b = [gb("ktm0"), gb("ktm1")]
        s3s = [{n_: gb(f"{n_}_{i}") for n_ in ["osc", "ssq", "srt", "rst"]} for i in range(2)]

        p.op(PO, lambda e: e.iota(iota_s[:, :], pattern=[[1, 128]], channel_multiplier=-1,
                                  allow_small_or_imprecise_dtypes=True), writes=[gb("iota_s")])
        p.op(DV, lambda e: e.tensor_scalar(maskT[:, :], iota_s[:, :], 0.0, None, op0=ALU.is_ge),
             reads=[gb("iota_s")], writes=[gb("maskT")])
        p.op(PO, lambda e: e.memset(eps5[:, :], 1e-5), writes=[gb("eps5")])
        p.op(PO, lambda e: e.memset(eps8[:, :], 1e-8), writes=[gb("eps8")])
        p.op(PO, lambda e: e.memset(state_f[:, :], 0.0), writes=sf_h)
        for cname, src_, scl in [("bq1", W["fq1b"], SCALE), ("bq2", W["fq2b"], None),
                                 ("bk1", W["fk1b"], None), ("bk2", W["fk2b"], None),
                                 ("nw", W["nw"], None)]:
            bb = gb("c_" + cname)
            dst = bcol[cname]
            dma_pg(dst[:, :], src_.rearrange("(p one) -> p one", one=1), writes=[bb])
            if scl is not None:
                p.op(DV, lambda e, d=dst, s=scl: e.tensor_scalar(d[:, :], d[:, :], s, None,
                                                                 op0=ALU.mult),
                     reads=[bb], writes=[bb])
        for fname in ["q1", "q2", "k1", "k2"]:
            bs, bt = gb("fstage"), gb("fT_" + fname)
            dma_pg(fstage[:, :], W["f" + fname], writes=[bs])
            dma_tr(fT[fname][:, 0:128], fstage[:, :], reads=[bs], writes=[bt])

        sS1 = contextlib.ExitStack()
        k2fm = sb(sS1, "k2fm", H * T, BF16)
        vtm = sb(sS1, "vtm", 8 * D, BF16)
        k2_h = [gb(f"k2_{h}") for h in range(H)]
        vtm_t = [gb(f"vtm{i}") for i in range(8)]
        k2_v = k2fm[:, :].rearrange("p (pr hg h4 c) -> p pr hg h4 c", hg=4, h4=4, c=128)
        vtm_v = vtm[:, :].rearrange("p (tt j) -> p tt j", j=D)

        wkTc_b = [gb(f"wkTc{h}") for h in range(H)]
        wvTc_b = [gb(f"wvTc{j}") for j in range(4)]
        sS1b = contextlib.ExitStack()
        wvT = sb(sS1b, "wvT", H * 512, BF16)
        wvT_b = gb("wvT")
        wvT_v = wvT[:, :].rearrange("p (d j) -> p d j", j=512)

        sSA = contextlib.ExitStack()
        xbf = sb(sSA, "xbf", H * T, BF16, side="right")
        xrow = [sb(sSA, f"xrow{i}", 2 * D, BF16, side="right") for i in range(1)]
        xbf_t = [gb(f"xbf{i}") for i in range(8)]
        xrow_b = [gb("xrow0"), gb("xrow1")]
        xbf_v = xbf[:, :].rearrange("p (d t) -> p d t", t=T)

        def proj_prep(w_dram, h, cache=None, cached=False):
            r2 = h % 2
            rp = (h // 2) % 2
            wTv = wTh[r2][:, :].rearrange("p (d j) -> p d j", j=128)
            if cached:
                dma_ds(wTh[r2][:, :], cache[h], reads=[wkTc_b[h]], writes=[wTh_b[r2]])
            else:
                if h % 2 == 0:
                    dma_pg(wrow[rp][:, :].rearrange("p (a d) -> p a d", d=D),
                           w_dram[h * 128:(h + 2) * 128, :].rearrange("(a p) d -> p a d", p=128),
                           writes=[wrow_b[rp]])
                dma_tr(wTv, wrow[rp][:, (h % 2) * D:(h % 2 + 1) * D],
                       reads=[wrow_b[rp]], writes=[wTh_b[r2]])
                if cache is not None:
                    dma_pg(cache[h], wTh[r2][:, :], reads=[wTh_b[r2]], writes=[wkTc_b[h]])

        def proj_head(w_dram, h, cache=None, cached=False):
            r2 = h % 2
            wTv = wTh[r2][:, :].rearrange("p (d j) -> p d j", j=128)
            for half in range(2):
                bank = 2 * r2 + half
                fns = [lambda e, dt=dt, bank=bank, half=half, wTv=wTv: e.matmul(
                    PS[bank][:, :], wTv[:, dt, :],
                    xbf_v[:, dt, half * 512:(half + 1) * 512],
                    start=(dt == 0), stop=(dt == 15)) for dt in range(16)]
                p.op(PE, fns, reads=[wTh_b[r2]] + xbf_t[half * 4:(half + 1) * 4],
                     writes=[psb[bank]])

        def proj_evac(h):
            r2 = h % 2
            for half in range(2):
                p.op(DV, lambda e, half=half, r2=r2, b=2 * r2 + half: e.tensor_copy(
                    kqf[r2][:, half * 512:(half + 1) * 512], PS[b][:, :]),
                    reads=[psb[2 * r2 + half]], writes=[kqf_b[r2]])

        def fmap(h, w1, w2, b1n, b2n, dst_sl, dst_buf, scale):
            r2 = h % 2
            for half in range(2):
                sl = slice(half * 512, (half + 1) * 512)
                p.op(PE, lambda e, r2=r2, sl=sl, w1=w1: e.matmul(
                    PS[6][:, :], fT[w1][:, 0:128], kqf[r2][:, sl], start=True, stop=True),
                    reads=[gb("fT_" + w1), kqf_b[r2]], writes=[psb[6]])
                p.op(PE, lambda e, r2=r2, sl=sl, w2=w2: e.matmul(
                    PS[7][:, :], fT[w2][:, 0:128], kqf[r2][:, sl], start=True, stop=True),
                    reads=[gb("fT_" + w2), kqf_b[r2]], writes=[psb[7]])
                p.op(AC, lambda e, half=half, scale=scale, b1n=b1n: e.activation(
                    t1s[half][:, :], PS[6][:, :], AF.Identity, bias=bcol[b1n][:, :], scale=scale),
                    reads=[psb[6], gb("c_" + b1n)], writes=[t1_b[half]])
                p.op(DV, lambda e, half=half, h=h, b2n=b2n, dst_sl=dst_sl: e.scalar_tensor_tensor(
                    dst_sl(h, half), PS[7][:, :],
                    bcol[b2n][:, :], t1s[half][:, :], op0=ALU.add, op1=ALU.mult),
                    reads=[psb[7], gb("c_" + b2n), t1_b[half]], writes=[dst_buf])

        def v_prep(js, cached=False):
            if cached:
                dma_ds(wvT[:, :], wvT_hbm[js], reads=[wvTc_b[js]], writes=[wvT_b])
            else:
                for jp in range(2):
                    r4 = 2
                    rows = slice(js * 512 + jp * 256, js * 512 + (jp + 1) * 256)
                    dma_pg(wrow[r4][:, :].rearrange("p (a d) -> p a d", d=D),
                           W["wv"][rows, :].rearrange("(a p) d -> p a d", p=128),
                           writes=[wrow_b[r4]])
                    for jj in range(2):
                        dma_tr(wvT_v[:, :, (2 * jp + jj) * 128:(2 * jp + jj + 1) * 128],
                               wrow[r4][:, jj * D:(jj + 1) * D],
                               reads=[wrow_b[r4]], writes=[wvT_b])
                dma_pg(wvT_hbm[js], wvT[:, :], reads=[wvT_b], writes=[wvTc_b[js]])

        def v_chains(js):
            for tt in range(8):
                bank = 4 + tt % 2
                fns = [lambda e, dt=dt, tt=tt, bank=bank: e.matmul(
                    PS[bank][:, :], xbf_v[:, dt, tt * 128:(tt + 1) * 128],
                    wvT_v[:, dt, :], start=(dt == 0), stop=(dt == 15)) for dt in range(16)]
                p.op(PE, fns, reads=[wvT_b, xbf_t[tt]], writes=[psb[bank]])
                p.op(DV, lambda e, tt=tt, js=js, bank=bank: e.tensor_copy(
                    vtm_v[:, tt, js * 512:(js + 1) * 512], PS[bank][:, :]),
                    reads=[psb[bank]], writes=[vtm_t[tt]])

        def k_sl(h, half):
            return k2_v[:, half * 4:(half + 1) * 4, h // 4, h % 4, :]

        def totals_block():
            for pr in range(8):
                r = pr % 2
                ktm_v = ktm[r][:, :].rearrange("p (hg h4 e) -> p hg h4 e", hg=4, h4=4, e=128)
                dma_tr(ktm_v, k2_v[:, pr, :, :, :], reads=k2_h, writes=[ktm_b[r]])
                fns = []
                for h in range(H):
                    bank = 4 + h // 4
                    fns.append(lambda e, h=h, pr=pr, bank=bank, ktm_v=ktm_v: e.matmul(
                        PS[bank][:, (h % 4) * 128:(h % 4 + 1) * 128],
                        ktm_v[:, h // 4, h % 4, :],
                        vtm_v[:, pr, h * 128:(h + 1) * 128],
                        start=(h % 4 == 0), stop=(h % 4 == 3)))
                p.op(PE, fns, reads=[ktm_b[r]] + vtm_t,
                     writes=[psb[4], psb[5], psb[6], psb[7]])
                for h in range(H):
                    bank = 4 + h // 4
                    p.op(DV, lambda e, h=h, bank=bank: e.tensor_tensor(
                        state_f[:, h * 128:(h + 1) * 128],
                        PS[bank][:, (h % 4) * 128:(h % 4 + 1) * 128],
                        state_f[:, h * 128:(h + 1) * 128], op=ALU.add),
                        reads=[psb[bank], sf_h[h]], writes=[sf_h[h]])

        for blk in range(NB):
            for tp in range(4):
                r = 0
                rows = slice(blk * T + tp * 256, blk * T + (tp + 1) * 256)
                dma_pg(xrow[r][:, :].rearrange("p (a d) -> p a d", d=D),
                       xkv[rows, :].rearrange("(a p) d -> p a d", p=128),
                       writes=[xrow_b[r]])
                for ji in range(2):
                    tt = 2 * tp + ji
                    dma_tr(xbf_v[:, :, tt * 128:(tt + 1) * 128],
                           xrow[r][:, ji * D:(ji + 1) * D],
                           reads=[xrow_b[r]], writes=[xbf_t[tt]])

            proj_prep(W["wk"], 0, cache=wkT_hbm, cached=(blk > 0))
            proj_prep(W["wk"], 1, cache=wkT_hbm, cached=(blk > 0))
            for h in range(H):
                if h % 4 == 0:
                    v_prep(h // 4, cached=(blk > 0))
                proj_head(W["wk"], h)
                if h + 2 < H:
                    proj_prep(W["wk"], h + 2, cache=wkT_hbm, cached=(blk > 0))
                proj_evac(h)
                if h > 0:
                    fmap(h - 1, "k1", "k2", "bk1", "bk2", k_sl, k2_h[h - 1], 1.0)
                if h % 4 == 3:
                    v_chains(h // 4)
            fmap(H - 1, "k1", "k2", "bk1", "bk2", k_sl, k2_h[H - 1], 1.0)
            if blk < NB - 1:
                totals_block()

        sS1b.close()

        sQ = contextlib.ExitStack()
        q2s = sb(sQ, "q2s", H * T, BF16)
        q2_h = [gb(f"q2_{h}") for h in range(H)]
        q2_v = q2s[:, :].rearrange("p (pr h c) -> p pr h c", h=H, c=128)

        def q_sl(h, half):
            return q2_v[:, half * 4:(half + 1) * 4, h, :]
        proj_prep(W["wq"], 0)
        proj_prep(W["wq"], 1)
        for h in range(H):
            proj_head(W["wq"], h)
            if h + 2 < H:
                proj_prep(W["wq"], h + 2)
            proj_evac(h)
            if h > 0:
                fmap(h - 1, "q1", "q2", "bq1", "bq2", q_sl, q2_h[h - 1], SCALE)
        fmap(H - 1, "q1", "q2", "bq1", "bq2", q_sl, q2_h[H - 1], SCALE)

        if "dbg_k2" in dbg:
            dma_pg(dbg["dbg_k2"].rearrange("p (pr hg h4 c) -> p pr hg h4 c", hg=4, h4=4, c=128), k2_v, reads=k2_h)
        if "dbg_v" in dbg:
            dma_pg(dbg["dbg_v"].rearrange("p (t j) -> p t j", j=D), vtm_v, reads=vtm_t)
        if "dbg_q2" in dbg:
            dma_pg(dbg["dbg_q2"].rearrange("p (pr h c) -> p pr h c", h=H, c=128), q2_v, reads=q2_h)

        sSA.close()
        p.barrier()

        sS2 = contextlib.ExitStack()
        o_rms = sb(sS2, "o_rms", 8 * D, BF16, side="right")
        orms_pr = [gb(f"orms{i}") for i in range(8)]
        orms_v = o_rms[:, :].rearrange("p (t hd) -> p t hd", hd=D)

        if "dbg_tot" in dbg:
            dma_pg(dbg["dbg_tot"].rearrange("p (h e) -> p h e", e=128),
                   state_f[:, :].rearrange("p (h e) -> p h e", e=128), reads=sf_h)

        def passA(pr):
            r = pr % 2
            ktm_v = ktm[r][:, :].rearrange("p (hg h4 e) -> p hg h4 e", hg=4, h4=4, e=128)
            dma_tr(ktm_v, k2_v[:, pr, :, :, :], reads=k2_h, writes=[ktm_b[r]])
            mv2_ = msT2[r][:, :].rearrange("p (h c) -> p h c", c=128)
            for h in range(H):
                bank = h % 2
                p.op(PE, lambda e, h=h, pr=pr, bank=bank: e.matmul(
                    PS[bank][:, 0:128], k2_v[:, pr, h // 4, h % 4, :],
                    q2_v[:, pr, h, :], start=True, stop=True),
                    reads=[k2_h[h], q2_h[h]], writes=[psb[bank]])
                p.op(DV, lambda e, h=h, bank=bank, mv2_=mv2_: e.tensor_tensor(
                    mv2_[:, h, :], PS[bank][:, 0:128], maskT[:, :], op=ALU.mult),
                    reads=[psb[bank], gb("maskT")], writes=[msT_h[h]])

        def passB(pr):
            r = pr % 2
            ktm_v = ktm[r][:, :].rearrange("p (hg h4 e) -> p hg h4 e", hg=4, h4=4, e=128)
            mv2_ = msT2[r][:, :].rearrange("p (h c) -> p h c", c=128)
            for h in range(H):
                ob = [2, 3, 6, 7][h % 4]
                kb = 4 + h % 2
                p.op(AC, lambda e, h=h: e.copy(
                    sbr[h % 4][:, :], state_f[:, h * 128:(h + 1) * 128]),
                    reads=[sf_h[h]], writes=[sbf_h[h]])
                fns = [
                    lambda e, h=h, pr=pr, ob=ob, mv2_=mv2_: e.matmul(
                        PS[ob][:, 0:128], mv2_[:, h, :], vtm_v[:, pr, h * 128:(h + 1) * 128],
                        start=True, stop=False),
                    lambda e, h=h, pr=pr, ob=ob: e.matmul(
                        PS[ob][:, 0:128], q2_v[:, pr, h, :],
                        sbr[h % 4][:, :], start=False, stop=True),
                ]
                p.op(PE, fns, reads=[msT_h[h], vtm_t[pr], q2_h[h], sbf_h[h]], writes=[psb[ob]])
                p.op(PE, lambda e, h=h, pr=pr, kb=kb, ktm_v=ktm_v: e.matmul(
                    PS[kb][:, 0:128], ktm_v[:, h // 4, h % 4, :],
                    vtm_v[:, pr, h * 128:(h + 1) * 128],
                    start=True, stop=True),
                    reads=[ktm_b[r], vtm_t[pr]], writes=[psb[kb]])
                p.op(DV, lambda e, h=h, kb=kb: e.tensor_tensor(
                    state_f[:, h * 128:(h + 1) * 128], PS[kb][:, 0:128],
                    state_f[:, h * 128:(h + 1) * 128], op=ALU.add),
                    reads=[psb[kb], sf_h[h], sbf_h[h]], writes=[sf_h[h]])
                rr2 = h % 2
                p.op(AC, lambda e, ob=ob, rr2=rr2: e.activation(
                    osc2[rr2][:, :], PS[ob][:, 0:128], AF.Square, accum_out=ssq2[rr2][:, :]),
                    reads=[psb[ob]], writes=[s3s[rr2]["osc"], s3s[rr2]["ssq"]])
                p.op(AC, lambda e, rr2=rr2: e.activation(
                    srt2[rr2][:, :], ssq2[rr2][:, :], AF.Sqrt, bias=eps5[:, :], scale=1.0 / DK),
                    reads=[s3s[rr2]["ssq"], gb("eps5")], writes=[s3s[rr2]["srt"]])
                p.op(DV, lambda e, rr2=rr2: e.reciprocal(rst2[rr2][:, :], srt2[rr2][:, :]),
                     reads=[s3s[rr2]["srt"]], writes=[s3s[rr2]["rst"]])
                p.op(DV, lambda e, h=h, pr=pr, ob=ob, rr2=rr2: e.tensor_scalar(
                    orms_v[:, pr, h * 128:(h + 1) * 128], PS[ob][:, 0:128],
                    rst2[rr2][:, :], None, op0=ALU.mult),
                    reads=[psb[ob], s3s[rr2]["rst"]], writes=[orms_pr[pr]])

        passA(0)
        for pr in range(1, 8):
            passA(pr)
            passB(pr - 1)
        passB(7)
        if "dbg_orms" in dbg:
            dma_pg(dbg["dbg_orms"].rearrange("p (t j) -> p t j", j=D), orms_v, reads=orms_pr)

        sQ.close()
        sS1.close()
        sSR.close()
        p.barrier()

        sS2b = contextlib.ExitStack()
        o_fm = sb(sS2b, "o_fm", H * T, BF16, side="right")
        ofm_t = [gb(f"ofm{i}") for i in range(8)]
        ofm_v = o_fm[:, :].rearrange("p (h t) -> p h t", t=T)
        for pr in range(8):
            outap = ofm_v[:, :, pr * 128:(pr + 1) * 128]
            dma_tr(outap, orms_v[:, pr, :], reads=[orms_pr[pr]], writes=[ofm_t[pr]])
            p.op(DV, lambda e, outap=outap: e.tensor_scalar(
                outap, outap, bcol["nw"][:, :], None, op0=ALU.mult),
                reads=[ofm_t[pr], gb("c_nw")], writes=[ofm_t[pr]])

        sS3 = contextlib.ExitStack()
        u_sb = sb(sS3, "u_sb", 8 * D, F32)
        sS3a = contextlib.ExitStack()
        woT = [sb(sS3a, f"woT{i}", H * 512, BF16) for i in range(2)]
        xres = [sb(sS3a, f"xres{i}", 512, F32) for i in range(2)]
        woT_b = [gb("woT0"), gb("woT1")]
        xres_b = [gb("xres0"), gb("xres1")]
        u_t = [gb(f"u_t{i}") for i in range(8)]
        u_v = u_sb[:, :].rearrange("p (tt o) -> p tt o", o=D)
        stats5_v = stats5[:, :].rearrange("p (t o s) -> p t o s", o=4, s=6)
        st5_t = [gb(f"st5_{i}") for i in range(8)]
        def wo_prep(os_):
            so = os_ % 2
            woT_v = woT[so][:, :].rearrange("p (j o) -> p j o", o=512)
            for jp in range(2):
                r = jp % 2
                rows = slice(os_ * 512 + jp * 256, os_ * 512 + (jp + 1) * 256)
                dma_pg(wrow[r][:, :].rearrange("p (a d) -> p a d", d=D),
                       W["wo"][rows, :].rearrange("(a p) d -> p a d", p=128),
                       writes=[wrow_b[r]])
                for jj in range(2):
                    dma_tr(woT_v[:, :, (2 * jp + jj) * 128:(2 * jp + jj + 1) * 128],
                           wrow[r][:, jj * D:(jj + 1) * D],
                           reads=[wrow_b[r]], writes=[woT_b[so]])
            return woT_v

        wo_prep(0)
        wo_prep(1)
        for os_ in range(4):
            so = os_ % 2
            woT_v = woT[so][:, :].rearrange("p (j o) -> p j o", o=512)
            if os_ + 2 < 4:
                pass
            for tt in range(8):
                bank = tt % 4
                fns = [lambda e, jt=jt, tt=tt, bank=bank, woT_v=woT_v: e.matmul(
                    PS[bank][:, :], ofm_v[:, jt, tt * 128:(tt + 1) * 128],
                    woT_v[:, jt, :], start=(jt == 0), stop=(jt == 15)) for jt in range(16)]
                p.op(PE, fns, reads=[woT_b[so]] + ofm_t, writes=[psb[bank]])
                xr = tt % 2
                dma_pg(xres[xr][:, :],
                       xkv[3 * T + tt * 128:3 * T + (tt + 1) * 128, os_ * 512:(os_ + 1) * 512],
                       writes=[xres_b[xr]])
                p.op(DV, lambda e, bank=bank, xr=xr, tt=tt, os_=os_: e.tensor_tensor(
                    u_v[:, tt, os_ * 512:(os_ + 1) * 512], PS[bank][:, :], xres[xr][:, :],
                    op=ALU.add),
                    reads=[psb[bank], xres_b[xr]], writes=[u_t[tt]])
                p.op(DV, lambda e, tt=tt, os_=os_: e.bn_stats(
                    stats5_v[:, tt, os_, :], u_v[:, tt, os_ * 512:(os_ + 1) * 512]),
                    reads=[u_t[tt]], writes=[st5_t[tt]])
                if tt == 7 and os_ + 2 < 4:
                    wo_prep(os_ + 2)

        sS2b.close()
        sS2.close()
        sS3a.close()
        p.barrier()

        sS4a = contextlib.ExitStack()
        yr = [sb(sS4a, f"yr{i}", D, F32) for i in range(2)]
        yr_b = [gb("yr0"), gb("yr1")]
        zr = sb(sS4a, "zr", D, F32)
        tr2 = [sb(sS4a, f"tr2_{i}", 1024, F32) for i in range(2)]
        dr2 = [sb(sS4a, f"dr2_{i}", 1024, F32) for i in range(2)]
        i32r2 = [sb(sS4a, f"i32r2_{i}", 1024, I32) for i in range(2)]
        kfr2 = [sb(sS4a, f"kfr2_{i}", 1024, F32) for i in range(2)]
        xqr2 = [sb(sS4a, f"xqr2_{i}", 1024, BF16) for i in range(2)]
        tb2 = [gb("tr2_0"), gb("tr2_1")]
        db2 = [gb("dr2_0"), gb("dr2_1")]
        i32b2 = [gb("i32r2_0"), gb("i32r2_1")]
        kfb2 = [gb("kfr2_0"), gb("kfr2_1")]
        xqb2 = [gb("xqr2_0"), gb("xqr2_1")]
        rmaxc = sb(sS4a, "rmaxc", 1, F32)
        nmc = [sb(sS4a, f"nmc{i}", 1, F32) for i in range(2)]
        ysum = sb(sS4a, "ysum", 1, F32)
        ysq = sb(sS4a, "ysq", 1, F32)
        mvv = sb(sS4a, "mvv", 1, F32)
        oscr6 = sb(sS4a, "oscr6", D, F32)
        ysum_b, ysq_b, mvv_b, oscr6_b = gb("ysum"), gb("ysq"), gb("mvv"), gb("oscr6")
        rcc = sb(sS4a, "rcc", 1, F32)
        scc = sb(sS4a, "scc", 1, F32)
        rscc = sb(sS4a, "rscc", 1, F32)
        sS5 = contextlib.ExitStack()
        xq_fm = sb(sS5, "xq_fm", H * T, BF16, side="right")
        xq_t = [gb(f"xq{i}") for i in range(8)]
        xq_v = xq_fm[:, :].rearrange("p (d t) -> p d t", t=T)
        zb = gb("zr")
        nmb = [gb("nmc0"), gb("nmc1")]
        rmaxb, rccb, sccb, rscb = gb("rmaxc"), gb("rcc"), gb("scc"), gb("rscc")
        yhbm_t = [gb(f"yhbm{i}") for i in range(8)]

        for tt in range(8):
            r = tt % 2
            p.op(DV, lambda e, tt=tt: e.bn_aggr(
                mv[:, :], stats5_v[:, tt, :, :].rearrange("p o s -> p (o s)")),
                reads=[st5_t[tt]], writes=[scr["mv"]])
            p.op(AC, lambda e: e.activation(srt[:, :], mv[:, 1:2], AF.Sqrt, bias=eps5[:, :]),
                 reads=[scr["mv"], gb("eps5")], writes=[scr["srt"]])
            p.op(DV, lambda e: e.reciprocal(rstd[:, :], srt[:, :]),
                 reads=[scr["srt"]], writes=[scr["rstd"]])
            p.op(DV, lambda e: e.scalar_tensor_tensor(
                nmc[0][:, :], mv[:, 0:1], -1.0, rstd[:, :], op0=ALU.mult, op1=ALU.mult),
                reads=[scr["mv"], scr["rstd"]], writes=[nmb[0]])
            p.op(AC, lambda e, r=r, tt=tt: e.activation(
                yr[r][:, :], u_v[:, tt, :], AF.Identity, bias=nmc[0][:, :], scale=rstd[:, :],
                accum_out=ysum[:, :]),
                reads=[u_t[tt], nmb[0], scr["rstd"]], writes=[yr_b[r], ysum_b])
            dma_pg(y_hbm[tt * 128:(tt + 1) * 128, :], yr[r][:, :],
                   reads=[yr_b[r]], writes=[yhbm_t[tt]])
            for i in range(4):
                p.op(DV, lambda e, r=r, i=i: e.bn_stats(
                    bnst[:, i * 6:(i + 1) * 6], yr[r][:, i * 512:(i + 1) * 512]),
                    reads=[yr_b[r]], writes=[scr["bnst"]])
            p.op(DV, lambda e: e.bn_aggr(mv[:, :], bnst[:, :]),
                 reads=[scr["bnst"]], writes=[scr["mv"]])
            p.op(AC, lambda e: e.activation(srt[:, :], mv[:, 1:2], AF.Sqrt, bias=eps8[:, :]),
                 reads=[scr["mv"], gb("eps8")], writes=[scr["srt"]])
            p.op(DV, lambda e: e.reciprocal(rstd[:, :], srt[:, :]),
                 reads=[scr["srt"]], writes=[scr["rstd"]])
            p.op(DV, lambda e: e.scalar_tensor_tensor(
                nmc[1][:, :], mv[:, 0:1], -1.0, rstd[:, :], op0=ALU.mult, op1=ALU.mult),
                reads=[scr["mv"], scr["rstd"]], writes=[nmb[1]])
            p.op(AC, lambda e, r=r: e.activation(
                zr[:, :], yr[r][:, :], AF.Identity, bias=nmc[1][:, :], scale=rstd[:, :]),
                reads=[yr_b[r], nmb[1], scr["rstd"]], writes=[zb])
            p.op(DV, lambda e: e.reduce_max(rmaxc[:, :], zr[:, :], axis=AX.X,
                                            apply_absolute_value=True),
                 reads=[zb], writes=[rmaxb])
            p.op(DV, lambda e: e.reciprocal(rcc[:, :], rmaxc[:, :]), reads=[rmaxb], writes=[rccb])
            p.op(DV, lambda e: e.tensor_scalar(scc[:, :], rcc[:, :], 127.0, None, op0=ALU.mult),
                 reads=[rccb], writes=[sccb])
            p.op(DV, lambda e: e.tensor_scalar(rscc[:, :], rmaxc[:, :], 1.0 / 127.0, None,
                                               op0=ALU.mult),
                 reads=[rmaxb], writes=[rscb])
            for hf in range(2):
                sl = slice(hf * 1024, (hf + 1) * 1024)
                tbh, dbh, i32bh, kfbh, xqbh = tb2[hf], db2[hf], i32b2[hf], kfb2[hf], xqb2[hf]
                p.op(AC, lambda e, sl=sl, hf=hf: e.activation(
                    tr2[hf][:, :], zr[:, sl], AF.Identity, scale=scc[:, :]),
                    reads=[zb, sccb], writes=[tbh])
                p.op(DV, lambda e, hf=hf: e.tensor_copy(i32r2[hf][:, :], tr2[hf][:, :]),
                     reads=[tbh], writes=[i32bh])
                p.op(DV, lambda e, hf=hf: e.tensor_copy(kfr2[hf][:, :], i32r2[hf][:, :]),
                     reads=[i32bh], writes=[kfbh])
                p.op(DV, lambda e, hf=hf: e.tensor_tensor(
                    dr2[hf][:, :], tr2[hf][:, :], kfr2[hf][:, :], op=ALU.subtract),
                    reads=[tbh, kfbh], writes=[dbh])
                p.op(DV, lambda e, hf=hf: e.scalar_tensor_tensor(
                    tr2[hf][:, :], dr2[hf][:, :], 0.5, kfr2[hf][:, :],
                    op0=ALU.is_ge, op1=ALU.add),
                    reads=[dbh, kfbh], writes=[tbh])
                p.op(DV, lambda e, hf=hf: e.scalar_tensor_tensor(
                    kfr2[hf][:, :], dr2[hf][:, :], -0.5, tr2[hf][:, :],
                    op0=ALU.is_gt, op1=ALU.add),
                    reads=[dbh, tbh], writes=[kfbh])
                p.op(DV, lambda e, hf=hf: e.tensor_scalar(
                    dr2[hf][:, :], kfr2[hf][:, :], 1.0, 127.0, op0=ALU.subtract, op1=ALU.min),
                    reads=[kfbh], writes=[dbh])
                p.op(DV, lambda e, hf=hf: e.tensor_scalar(
                    xqr2[hf][:, :], dr2[hf][:, :], -128.0, rscc[:, :],
                    op0=ALU.max, op1=ALU.mult),
                    reads=[dbh, rscb], writes=[xqbh])
                dma_tr(xq_v[:, hf * 8:(hf + 1) * 8, tt * 128:(tt + 1) * 128],
                       xqr2[hf][:, :], reads=[xqbh], writes=[xq_t[tt]])

        if "dbg_xq" in dbg:
            dma_pg(dbg["dbg_xq"].rearrange("p (d t) -> p d t", t=T), xq_v, reads=xq_t)

        sS4a.close()
        sS3.close()
        p.barrier()

        sAcc = contextlib.ExitStack()
        acc = sb(sAcc, "acc", 4 * D, F32)
        sS6 = contextlib.ExitStack()
        ringg = [sb(sS6, f"ringg{i}", H * 128, BF16) for i in range(2)]
        ringu = [sb(sS6, f"ringu{i}", H * 128, BF16) for i in range(2)]
        gsb = [sb(sS6, f"gsb{i}", 512, F32) for i in range(2)]
        pgr = [sb(sS6, f"pgr{i}", 8 * 512, BF16) for i in range(2)]
        sgn = [sb(sS6, f"sgn{i}", 2 * D, BF16) for i in range(2)]
        wds = [sb(sS6, f"wds{i}", 2 * 1024, BF16) for i in range(2)]
        wdsg = [sb(sS6, f"wdsg{i}", 2 * 1024, BF16) for i in range(2)]
        wdT = [sb(sS6, f"wdT{i}", 8 * 512, BF16) for i in range(2)]
        yr2 = sb(sS6, "yr2", D, F32)
        u2b = sb(sS6, "u2b", D, F32)
        outr = sb(sS6, "outr", D, F32)
        srt8 = [sb(sS6, f"srt8_{i}", 1, F32) for i in range(2)]
        rst8 = [sb(sS6, f"rst8_{i}", 1, F32) for i in range(2)]
        rg_b = [gb("ringg0"), gb("ringg1")]
        ru_b = [gb("ringu0"), gb("ringu1")]
        gsb_b = [gb("gsb0"), gb("gsb1")]
        pgr_b = [[gb(f"pgr{par}_{rc}") for rc in range(8)] for par in range(2)]
        sgn_b = [gb("sgn0"), gb("sgn1")]
        wds_b = [gb("wds0"), gb("wds1")]
        wdsg_b = [gb("wdsg0"), gb("wdsg1")]
        wdT_b = [gb("wdT0"), gb("wdT1")]
        acc_b = [gb(f"acc{i}") for i in range(4)]
        yr2_b, u2b_b, outr_b = gb("yr2"), gb("u2b"), gb("outr")
        srt8_b = [gb("srt8_0"), gb("srt8_1")]
        rst8_b = [gb("rst8_0"), gb("rst8_1")]
        wgc_b = [gb(f"wgc{i}") for i in range(64)]
        wuc_b = [gb(f"wuc{i}") for i in range(64)]
        wdc_b = [gb(f"wdc{i}") for i in range(32)]
        acc_v = acc[:, :].rearrange("p (tt o) -> p tt o", o=D)

        def ring_prep(rg_, rc, th):
            rr = rc % 2
            if th == 0:
                if rc % 2 == 0:
                    rows = slice(rg_ * 1024 + rc * 128, rg_ * 1024 + (rc + 2) * 128)
                    for (wname, wi, sgi) in (("wg", 0, 0), ("wu", 2, 1)):
                        dma_pg(wrow[wi][:, :].rearrange("p (a d) -> p a d", d=D),
                               W[wname][rows, :].rearrange("(a p) d -> p a d", p=128),
                               writes=[wrow_b[wi]])
                        p.op(AC, lambda e, wi=wi, sgi=sgi: e.sign(sgn[sgi][:, :], wrow[wi][:, :]),
                             reads=[wrow_b[wi]], writes=[sgn_b[sgi]])
                for (ring, ring_b, sgi, cache, cb) in (
                        (ringg, rg_b, 0, wgT_hbm, wgc_b), (ringu, ru_b, 1, wuT_hbm, wuc_b)):
                    rv = ring[rr][:, :].rearrange("p (d j) -> p d j", j=128)
                    dma_tr(rv, sgn[sgi][:, (rc % 2) * D:(rc % 2 + 1) * D],
                           reads=[sgn_b[sgi]], writes=[ring_b[rr]])
                    dma_pg(cache[rg_, rc], ring[rr][:, :], reads=[ring_b[rr]],
                           writes=[cb[rg_ * 8 + rc]])
            else:
                dma_ds(ringg[rr][:, :], wgT_hbm[rg_, rc], reads=[wgc_b[rg_ * 8 + rc]],
                       writes=[rg_b[rr]])
                dma_ds(ringu[rr][:, :], wuT_hbm[rg_, rc], reads=[wuc_b[rg_ * 8 + rc]],
                       writes=[ru_b[rr]])

        def wd_prep(rg_, os_, th):
            so = os_ % 2
            wdT_v = wdT[so][:, :].rearrange("p (rt o) -> p rt o", o=512)
            if th == 0:
                for op_ in range(2):
                    rr = op_ % 2
                    orow = os_ * 512 + op_ * 256
                    dma_pg(wds[rr][:, :].rearrange("p (a d) -> p a d", d=1024),
                           W["wd"][orow:orow + 256, rg_ * 1024:(rg_ + 1) * 1024]
                           .rearrange("(a p) d -> p a d", p=128),
                           writes=[wds_b[rr]])
                    p.op(AC, lambda e, rr=rr: e.sign(wdsg[rr][:, :], wds[rr][:, :]),
                         reads=[wds_b[rr]], writes=[wdsg_b[rr]])
                    for oo2 in range(2):
                        oo = 2 * op_ + oo2
                        dma_tr(wdT_v[:, :, oo * 128:(oo + 1) * 128],
                               wdsg[rr][:, oo2 * 1024:(oo2 + 1) * 1024],
                               reads=[wdsg_b[rr]], writes=[wdT_b[so]])
                dma_pg(wdT_hbm[rg_, os_], wdT[so][:, :], reads=[wdT_b[so]],
                       writes=[wdc_b[rg_ * 4 + os_]])
            else:
                dma_ds(wdT[so][:, :], wdT_hbm[rg_, os_], reads=[wdc_b[rg_ * 4 + os_]],
                       writes=[wdT_b[so]])
            return wdT_v

        for th in range(2):
            ring_prep(0, 0, th)
            ring_prep(0, 1, th)
            for rg_ in range(8):
                par = rg_ % 2
                pgr_v = pgr[par][:, :].rearrange("p (rt t) -> p rt t", t=512)
                for rc in range(8):
                    rr = rc % 2
                    gbank, ubank = rr, 2 + rr
                    for (bank, ring, ring_b) in ((gbank, ringg, rg_b), (ubank, ringu, ru_b)):
                        rv = ring[rr][:, :].rearrange("p (d j) -> p d j", j=128)
                        fns = [lambda e, dt=dt, bank=bank, rv=rv, th=th: e.matmul(
                            PS[bank][:, :], rv[:, dt, :],
                            xq_v[:, dt, th * 512:(th + 1) * 512],
                            start=(dt == 0), stop=(dt == 15)) for dt in range(16)]
                        p.op(PE, fns, reads=[ring_b[rr]] + xq_t[th * 4:(th + 1) * 4],
                             writes=[psb[bank]])
                    p.op(AC, lambda e, rr=rr, gbank=gbank: e.activation(
                        gsb[rr][:, :], PS[gbank][:, :], AF.Silu),
                        reads=[psb[gbank]], writes=[gsb_b[rr]])
                    p.op(DV, lambda e, rr=rr, rc=rc, ubank=ubank, pgr_v=pgr_v: e.tensor_tensor(
                        pgr_v[:, rc, :], PS[ubank][:, :], gsb[rr][:, :], op=ALU.mult),
                        reads=[psb[ubank], gsb_b[rr]], writes=[pgr_b[par][rc]])
                    nc_, ng_ = rc + 2, rg_
                    if nc_ >= 8:
                        nc_, ng_ = nc_ - 8, rg_ + 1
                    if ng_ < 8:
                        ring_prep(ng_, nc_, th)
                for os_ in range(4):
                    wdT_v = wd_prep(rg_, os_, th)
                    so = os_ % 2
                    for tl in range(4):
                        tt = th * 4 + tl
                        bank = 4 + tl
                        fns = [lambda e, rt=rt, tl=tl, bank=bank, wdT_v=wdT_v, pgr_v=pgr_v:
                               e.matmul(
                            PS[bank][:, :], pgr_v[:, rt, tl * 128:(tl + 1) * 128],
                            wdT_v[:, rt, :], start=(rt == 0), stop=(rt == 7)) for rt in range(8)]
                        p.op(PE, fns, reads=[wdT_b[so]] + pgr_b[par], writes=[psb[bank]])
                        if rg_ == 0:
                            p.op(DV, lambda e, bank=bank, tl=tl, os_=os_: e.tensor_copy(
                                acc_v[:, tl, os_ * 512:(os_ + 1) * 512], PS[bank][:, :]),
                                reads=[psb[bank]], writes=[acc_b[tl]])
                        else:
                            p.op(DV, lambda e, bank=bank, tl=tl, os_=os_: e.tensor_tensor(
                                acc_v[:, tl, os_ * 512:(os_ + 1) * 512], PS[bank][:, :],
                                acc_v[:, tl, os_ * 512:(os_ + 1) * 512], op=ALU.add),
                                reads=[psb[bank], acc_b[tl]], writes=[acc_b[tl]])
            for tl in range(4):
                tt = th * 4 + tl
                dma_pg(yr2[:, :], y_hbm[tt * 128:(tt + 1) * 128, :],
                       reads=[yhbm_t[tt]], writes=[yr2_b])
                p.op(DV, lambda e, tl=tl: e.tensor_tensor(
                    u2b[:, :], acc_v[:, tl, :], yr2[:, :], op=ALU.add),
                    reads=[acc_b[tl], yr2_b], writes=[u2b_b])
                r = tl % 2
                bn_, mv_ = (bnst, mv) if r == 0 else (bnst2, mv2)
                bnb_, mvb_ = (gb("bnst"), gb("mv")) if r == 0 else (gb("bnst2"), gb("mv2"))
                for i in range(4):
                    p.op(DV, lambda e, i=i, bn_=bn_: e.bn_stats(
                        bn_[:, i * 6:(i + 1) * 6], u2b[:, i * 512:(i + 1) * 512]),
                        reads=[u2b_b], writes=[bnb_])
                p.op(DV, lambda e, bn_=bn_, mv_=mv_: e.bn_aggr(mv_[:, :], bn_[:, :]),
                     reads=[bnb_], writes=[mvb_])
                p.op(AC, lambda e, mv_=mv_, r=r: e.activation(
                    srt8[r][:, :], mv_[:, 1:2], AF.Sqrt, bias=eps5[:, :]),
                     reads=[mvb_, gb("eps5")], writes=[srt8_b[r]])
                p.op(DV, lambda e, r=r: e.reciprocal(rst8[r][:, :], srt8[r][:, :]),
                     reads=[srt8_b[r]], writes=[rst8_b[r]])
                p.op(DV, lambda e, mv_=mv_, r=r: e.tensor_scalar(
                    outr[:, :], u2b[:, :], mv_[:, 0:1], rst8[r][:, :],
                    op0=ALU.subtract, op1=ALU.mult),
                    reads=[u2b_b, mvb_, rst8_b[r]], writes=[outr_b])
                dma_pg(out[tt * 128:(tt + 1) * 128, :], outr[:, :], reads=[outr_b])

        sS5.close()
        sS6.close()
        sAcc.close()

        p.raw_wait(PO, "dg", p.cnt["dg"])

        p.finalize()

        @block.tensor
        def _(e):
            for f in p.ops[PE]:
                f(e, S)

        @block.vector
        def _(e):
            for f in p.ops[DV]:
                f(e, S)

        @block.scalar
        def _(e):
            for f in p.ops[AC]:
                f(e, S)

        @block.gpsimd
        def _(e):
            for f in p.ops[PO]:
                f(e, S)

        @block.sync
        def _(e):
            for f in p.ops[SY]:
                f(e, S)

    return nc


_NC_CACHE = None


def kernel(L0_attn_x, L0_attn_wq, L0_attn_wk, L0_attn_wv, L0_attn_wo,
           L0_attn_fq1_w, L0_attn_fq1_b, L0_attn_fq2_w, L0_attn_fq2_b,
           L0_attn_fk1_w, L0_attn_fk1_b, L0_attn_fk2_w, L0_attn_fk2_b,
           L0_attn_norm_w, L0_ffn_W_g, L0_ffn_W_u, L0_ffn_W_d):
    x = np.ascontiguousarray(np.asarray(L0_attn_x, dtype=np.float32))
    ws = {n_: np.ascontiguousarray(np.asarray(a, dtype=np.float32)) for n_, a in [
        ("wq", L0_attn_wq), ("wk", L0_attn_wk), ("wv", L0_attn_wv), ("wo", L0_attn_wo),
        ("fq1", L0_attn_fq1_w), ("fq2", L0_attn_fq2_w),
        ("fk1", L0_attn_fk1_w), ("fk2", L0_attn_fk2_w),
        ("fq1b", L0_attn_fq1_b), ("fq2b", L0_attn_fq2_b),
        ("fk1b", L0_attn_fk1_b), ("fk2b", L0_attn_fk2_b),
        ("nw", L0_attn_norm_w),
        ("wg", L0_ffn_W_g), ("wu", L0_ffn_W_u), ("wd", L0_ffn_W_d)]}

    in_maps = []
    for c in range(NCORE):
        b, q = c // 4, c % 4
        n = T * (q + 1)
        xkv = np.zeros((NB * T, D), dtype=np.float32)
        xkv[NB * T - n:] = x[b, :n]
        m = {"xkv": xkv}
        m.update(ws)
        in_maps.append(m)

    global _NC_CACHE
    if _NC_CACHE is None:
        _NC_CACHE = build_nc()
    res = run_bass_kernel_spmd(_NC_CACHE, in_maps, core_ids=list(range(NCORE)))

    outa = np.empty((2, 4096, D), dtype=np.float32)
    for c in range(NCORE):
        b, q = c // 4, c % 4
        outa[b, q * T:(q + 1) * T] = res.results[c]["out"]
    if _DEBUG:
        kernel.debug = [res.results[c] for c in range(NCORE)]
    return outa


# version 18 (final)
# speedup vs baseline: 1.3091x; 1.3091x over previous
import contextlib
import numpy as np

import concourse.bass as bass
import concourse.mybir as mybir
from concourse.bass_utils import run_bass_kernel_spmd

F32 = mybir.dt.float32
BF16 = mybir.dt.bfloat16
I32 = mybir.dt.int32
AF = mybir.ActivationFunctionType
ALU = mybir.AluOpType
AX = mybir.AxisListType

H, DK, D, T, NB, R, NCORE = 16, 128, 2048, 1024, 4, 8192, 8
SCALE = float(DK) ** -0.5

PE, DV, AC, PO, SY = "pe", "dv", "ac", "po", "sy"
ENGS = (PE, DV, AC, PO, SY)
OWN_SEM = {PE: "pe", DV: "dv", AC: "ac", PO: "po", SY: "sy"}

_DEBUG = ()


class Buf:
    __slots__ = ("name", "writers", "readers")

    def __init__(self, name):
        self.name = name
        self.writers = {}
        self.readers = {}


class Prog:

    def __init__(self):
        self.ops = {e: [] for e in ENGS}
        self.cnt = {}
        self.waited = {e: {} for e in ENGS}
        self.fence = {}
        self.qeng = {"dg": PO, "dt": SY, "ds": AC}
        self.bounds = {"dg": {0}, "dt": {0}, "ds": {0}}
        self.tok_pos = {"dg": {0: 0}, "dt": {0: 0}, "ds": {0: 0}}
        self.pending_ins = {"dg": [], "dt": [], "ds": []}

    def ensure_boundary(self, s, c):
        if c in self.bounds[s]:
            return
        self.bounds[s].add(c)
        self.pending_ins[s].append((self.tok_pos[s][c], c))

    def finalize(self):
        for s, ins in self.pending_ins.items():
            eng = self.qeng[s]
            for idx, c in sorted(ins, key=lambda t: -t[0]):
                def emit(e, S, s=s, c=c):
                    e.wait_ge(S[s], c)
                self.ops[eng].insert(idx, emit)

    def barrier(self):
        for s, c in self.cnt.items():
            if self.fence.get(s, -1) < c:
                self.fence[s] = c
        for s in ("dg", "dt", "ds"):
            if s in self.cnt:
                self.ensure_boundary(s, self.cnt[s])

    def _collect(self, eng, writes, reads):
        need = dict(self.fence)
        for b in reads:
            for s, c in b.writers.items():
                if need.get(s, -1) < c:
                    need[s] = c
        for b in writes:
            for d in (b.writers, b.readers):
                for s, c in d.items():
                    if need.get(s, -1) < c:
                        need[s] = c
        wl = []
        for s, c in sorted(need.items()):
            if s == "pe" and eng == PE:
                continue
            if self.waited[eng].get(s, -1) < c:
                if s in self.bounds:
                    self.ensure_boundary(s, c)
                wl.append((s, c))
                self.waited[eng][s] = c
        return wl

    def op(self, eng, fns, reads=(), writes=(), inc=1, sem=None):
        if callable(fns):
            fns = [fns]
        sem = sem or OWN_SEM[eng]
        wl = self._collect(eng, writes, reads)
        n = self.cnt.get(sem, 0) + inc
        self.cnt[sem] = n
        if sem in self.tok_pos:
            self.tok_pos[sem][n] = len(self.ops[eng]) + 1

        def emit(e, S, wl=wl, fns=fns, sem=sem, inc=inc):
            for (s, c) in wl:
                e.wait_ge(S[s], c)
            ins = None
            for f in fns:
                ins = f(e)
            ins.then_inc(S[sem], inc)

        self.ops[eng].append(emit)
        for b in reads:
            if b.readers.get(sem, -1) < n:
                b.readers[sem] = n
        for b in writes:
            b.writers = {sem: n}
            b.readers = {}
        return (sem, n)

    def raw_wait(self, eng, sem, count):
        if sem in self.bounds:
            self.ensure_boundary(sem, count)

        def emit(e, S, sem=sem, count=count):
            e.wait_ge(S[sem], count)
        self.ops[eng].append(emit)


def build_nc():
    nc = bass.Bass("TRN2", target_bir_lowering=False)
    p = Prog()

    xkv = nc.dram_tensor("xkv", [NB * T, D], F32, kind="ExternalInput").ap()
    W = {}
    for n_, sh in [("wq", [D, D]), ("wk", [D, D]), ("wv", [D, D]), ("wo", [D, D]),
                   ("fq1", [DK, DK]), ("fq2", [DK, DK]), ("fk1", [DK, DK]), ("fk2", [DK, DK]),
                   ("fq1b", [DK]), ("fq2b", [DK]), ("fk1b", [DK]), ("fk2b", [DK]),
                   ("nw", [DK]), ("wg", [R, D]), ("wu", [R, D]), ("wd", [D, R])]:
        W[n_] = nc.dram_tensor(n_, sh, F32, kind="ExternalInput").ap()
    out = nc.dram_tensor("out", [T, D], F32, kind="ExternalOutput").ap()
    y_hbm = nc.dram_tensor("y_hbm", [T, D], F32).ap()
    wkT_hbm = nc.dram_tensor("wkT_hbm", [H, 128, H * 128], BF16).ap()
    wvT_hbm = nc.dram_tensor("wvT_hbm", [4, 128, H * 512], BF16).ap()

    dbg = {}
    for name, shape in [("dbg_k2", [128, H * T]), ("dbg_v", [128, 8 * D]),
                        ("dbg_tot", [128, H * DK]), ("dbg_q2", [128, H * T]),
                        ("dbg_orms", [128, 8 * D]), ("dbg_xq", [128, H * T]),
                        ("dbg_acc", [T, D]), ("dbg_yh", [T, D]),
                        ("dbg_p0", [128, 8 * T]), ("dbg_pall", [8, 128, 8 * T]), ("dbg_wdt", [128, 8 * 512])]:
        if name in _DEBUG:
            dbg[name] = nc.dram_tensor(name, shape, F32, kind="ExternalOutput").ap()

    B = {}

    def gb(name):
        if name not in B:
            B[name] = Buf(name)
        return B[name]

    root = contextlib.ExitStack()
    with root:
        S = {s: root.enter_context(nc.semaphore(s))
             for s in ["pe", "dv", "ac", "po", "sy", "dg", "dt", "ds"]}
        PS = [root.enter_context(nc.psum_tensor(f"ps{i}", [128, 512], F32)) for i in range(8)]
        psb = [gb(f"ps{i}") for i in range(8)]

        def sb(st, name, cols, dtype, side="left"):
            return st.enter_context(nc.sbuf_tensor(name, [128, cols], dtype, side=side))

        eps5 = sb(root, "eps5", 1, F32)
        eps8 = sb(root, "eps8", 1, F32)
        bcol = {n_: sb(root, "c_" + n_, 1, F32) for n_ in ["bq1", "bq2", "bk1", "bk2", "nw"]}
        wrow = [sb(root, f"wrow{i}", 2 * D, BF16) for i in range(3)]
        srt = sb(root, "srt", 1, F32)
        rstd = sb(root, "rstd", 1, F32)
        bnst = sb(root, "bnst", 24, F32)
        mv = sb(root, "mv", 2, F32)
        bnst2 = sb(root, "bnst2", 24, F32)
        mv2 = sb(root, "mv2", 2, F32)
        stats5 = sb(root, "stats5", 8 * 4 * 6, F32)

        wrow_b = [gb(f"wrow{i}") for i in range(3)]
        sf_h = [gb(f"sf{h}") for h in range(H)]
        sbf_h = [gb(f"sbf{h % 4}") for h in range(H)]
        msT_h = [gb(f"msT{h}") for h in range(H)]
        scr = {n_: gb(n_) for n_ in ["srt", "rstd", "mv", "bnst", "stats5"]}

        block = root.enter_context(nc.Block())

        def dma_pg(outap, inap, reads=(), writes=(), accum=None):
            if accum is None:
                fn = lambda e, o=outap, i=inap: e.dma_start(out=o, in_=i)
            else:
                fn = lambda e, o=outap, i=inap, a=accum: e.dma_start(out=o, in_=i, accum_op=a)
            return p.op(PO, fn, reads=reads, writes=writes, inc=16, sem="dg")

        def dma_ds(outap, inap, reads=(), writes=()):
            return p.op(AC, lambda e, o=outap, i=inap: e.dma_start(out=o, in_=i),
                        reads=reads, writes=writes, inc=16, sem="ds")

        def dma_tr(outap, inap, reads=(), writes=()):
            return p.op(SY, lambda e, o=outap, i=inap: e.dma_start_transpose(out=o, in_=i),
                        reads=reads, writes=writes, inc=16, sem="dt")

        sSR = contextlib.ExitStack()
        maskT = sb(sSR, "maskT", 128, F32)
        iota_s = sb(sSR, "iota_s", 128, F32)
        fT = {n_: sb(sSR, "fT_" + n_, 128, BF16) for n_ in ["q1", "q2", "k1", "k2"]}
        fstage = sb(sSR, "fstage", 128, BF16)
        state_f = sb(sSR, "state_f", H * DK, F32)
        sbr = [sb(sSR, f"sbr{i}", DK, BF16) for i in range(4)]
        wTh = [sb(sSR, f"wTh{i}", H * 128, BF16) for i in range(2)]
        kqf = [sb(sSR, f"kqf{i}", T, BF16) for i in range(2)]
        t1s = [sb(sSR, f"t1s{i}", 512, F32) for i in range(2)]
        ktm = [sb(sSR, f"ktm{i}", H * 128, BF16) for i in range(2)]
        msT2 = [sb(sSR, f"msT2_{i}", H * 128, BF16) for i in range(2)]
        osc2 = [sb(sSR, "osc2_0", 128, F32)] * 2
        ssq2 = [sb(sSR, f"ssq2_{i}", 1, F32) for i in range(2)]
        srt2 = [sb(sSR, f"srt2_{i}", 1, F32) for i in range(2)]
        rst2 = [sb(sSR, f"rst2_{i}", 1, F32) for i in range(2)]
        wTh_b = [gb("wTh0"), gb("wTh1")]
        kqf_b = [gb("kqf0"), gb("kqf1")]
        t1_b = [gb("t1s0"), gb("t1s1")]
        ktm_b = [gb("ktm0"), gb("ktm1")]
        s3s = [{n_: gb(f"{n_}_{i}") for n_ in ["osc", "ssq", "srt", "rst"]} for i in range(2)]

        p.op(PO, lambda e: e.iota(iota_s[:, :], pattern=[[1, 128]], channel_multiplier=-1,
                                  allow_small_or_imprecise_dtypes=True), writes=[gb("iota_s")])
        p.op(DV, lambda e: e.tensor_scalar(maskT[:, :], iota_s[:, :], 0.0, None, op0=ALU.is_ge),
             reads=[gb("iota_s")], writes=[gb("maskT")])
        p.op(PO, lambda e: e.memset(eps5[:, :], 1e-5), writes=[gb("eps5")])
        p.op(PO, lambda e: e.memset(eps8[:, :], 1e-8), writes=[gb("eps8")])
        p.op(PO, lambda e: e.memset(state_f[:, :], 0.0), writes=sf_h)
        for cname, src_, scl in [("bq1", W["fq1b"], SCALE), ("bq2", W["fq2b"], None),
                                 ("bk1", W["fk1b"], None), ("bk2", W["fk2b"], None),
                                 ("nw", W["nw"], None)]:
            bb = gb("c_" + cname)
            dst = bcol[cname]
            dma_pg(dst[:, :], src_.rearrange("(p one) -> p one", one=1), writes=[bb])
            if scl is not None:
                p.op(DV, lambda e, d=dst, s=scl: e.tensor_scalar(d[:, :], d[:, :], s, None,
                                                                 op0=ALU.mult),
                     reads=[bb], writes=[bb])
        for fname in ["q1", "q2", "k1", "k2"]:
            bs, bt = gb("fstage"), gb("fT_" + fname)
            dma_pg(fstage[:, :], W["f" + fname], writes=[bs])
            dma_tr(fT[fname][:, 0:128], fstage[:, :], reads=[bs], writes=[bt])

        sS1 = contextlib.ExitStack()
        k2fm = sb(sS1, "k2fm", H * T, BF16)
        vtm = sb(sS1, "vtm", 8 * D, BF16)
        k2_h = [gb(f"k2_{h}") for h in range(H)]
        vtm_t = [gb(f"vtm{i}") for i in range(8)]
        k2_v = k2fm[:, :].rearrange("p (pr hg h4 c) -> p pr hg h4 c", hg=4, h4=4, c=128)
        vtm_v = vtm[:, :].rearrange("p (tt j) -> p tt j", j=D)

        wkTc_b = [gb(f"wkTc{h}") for h in range(H)]
        wvTc_b = [gb(f"wvTc{j}") for j in range(4)]
        sS1b = contextlib.ExitStack()
        wvT = sb(sS1b, "wvT", H * 512, BF16)
        wvT_b = gb("wvT")
        wvT_v = wvT[:, :].rearrange("p (d j) -> p d j", j=512)

        sSA = contextlib.ExitStack()
        xbf = sb(sSA, "xbf", H * T, BF16, side="right")
        xrow = [sb(sSA, f"xrow{i}", 2 * D, BF16, side="right") for i in range(1)]
        xbf_t = [gb(f"xbf{i}") for i in range(8)]
        xrow_b = [gb("xrow0"), gb("xrow1")]
        xbf_v = xbf[:, :].rearrange("p (d t) -> p d t", t=T)

        def proj_prep(w_dram, h, cache=None, cached=False):
            r2 = h % 2
            rp = (h // 2) % 2
            wTv = wTh[r2][:, :].rearrange("p (d j) -> p d j", j=128)
            if cached:
                dma_pg(wTh[r2][:, :], cache[h], reads=[wkTc_b[h]], writes=[wTh_b[r2]])
            else:
                if h % 2 == 0:
                    dma_pg(wrow[rp][:, :].rearrange("p (a d) -> p a d", d=D),
                           w_dram[h * 128:(h + 2) * 128, :].rearrange("(a p) d -> p a d", p=128),
                           writes=[wrow_b[rp]])
                dma_tr(wTv, wrow[rp][:, (h % 2) * D:(h % 2 + 1) * D],
                       reads=[wrow_b[rp]], writes=[wTh_b[r2]])
                if cache is not None:
                    dma_pg(cache[h], wTh[r2][:, :], reads=[wTh_b[r2]], writes=[wkTc_b[h]])

        def proj_head(w_dram, h, cache=None, cached=False):
            r2 = h % 2
            wTv = wTh[r2][:, :].rearrange("p (d j) -> p d j", j=128)
            for half in range(2):
                bank = 2 * r2 + half
                fns = [lambda e, dt=dt, bank=bank, half=half, wTv=wTv: e.matmul(
                    PS[bank][:, :], wTv[:, dt, :],
                    xbf_v[:, dt, half * 512:(half + 1) * 512],
                    start=(dt == 0), stop=(dt == 15)) for dt in range(16)]
                p.op(PE, fns, reads=[wTh_b[r2]] + xbf_t[half * 4:(half + 1) * 4],
                     writes=[psb[bank]])

        def proj_evac(h):
            r2 = h % 2
            for half in range(2):
                p.op(DV, lambda e, half=half, r2=r2, b=2 * r2 + half: e.tensor_copy(
                    kqf[r2][:, half * 512:(half + 1) * 512], PS[b][:, :]),
                    reads=[psb[2 * r2 + half]], writes=[kqf_b[r2]])

        def fmap(h, w1, w2, b1n, b2n, dst_sl, dst_buf, scale):
            r2 = h % 2
            for half in range(2):
                sl = slice(half * 512, (half + 1) * 512)
                p.op(PE, lambda e, r2=r2, sl=sl, w1=w1: e.matmul(
                    PS[6][:, :], fT[w1][:, 0:128], kqf[r2][:, sl], start=True, stop=True),
                    reads=[gb("fT_" + w1), kqf_b[r2]], writes=[psb[6]])
                p.op(PE, lambda e, r2=r2, sl=sl, w2=w2: e.matmul(
                    PS[7][:, :], fT[w2][:, 0:128], kqf[r2][:, sl], start=True, stop=True),
                    reads=[gb("fT_" + w2), kqf_b[r2]], writes=[psb[7]])
                p.op(AC, lambda e, half=half, scale=scale, b1n=b1n: e.activation(
                    t1s[half][:, :], PS[6][:, :], AF.Identity, bias=bcol[b1n][:, :], scale=scale),
                    reads=[psb[6], gb("c_" + b1n)], writes=[t1_b[half]])
                p.op(DV, lambda e, half=half, h=h, b2n=b2n, dst_sl=dst_sl: e.scalar_tensor_tensor(
                    dst_sl(h, half), PS[7][:, :],
                    bcol[b2n][:, :], t1s[half][:, :], op0=ALU.add, op1=ALU.mult),
                    reads=[psb[7], gb("c_" + b2n), t1_b[half]], writes=[dst_buf])

        def v_prep(js, cached=False):
            if cached:
                dma_pg(wvT[:, :], wvT_hbm[js], reads=[wvTc_b[js]], writes=[wvT_b])
            else:
                for jp in range(2):
                    r4 = 2
                    rows = slice(js * 512 + jp * 256, js * 512 + (jp + 1) * 256)
                    dma_pg(wrow[r4][:, :].rearrange("p (a d) -> p a d", d=D),
                           W["wv"][rows, :].rearrange("(a p) d -> p a d", p=128),
                           writes=[wrow_b[r4]])
                    for jj in range(2):
                        dma_tr(wvT_v[:, :, (2 * jp + jj) * 128:(2 * jp + jj + 1) * 128],
                               wrow[r4][:, jj * D:(jj + 1) * D],
                               reads=[wrow_b[r4]], writes=[wvT_b])
                dma_pg(wvT_hbm[js], wvT[:, :], reads=[wvT_b], writes=[wvTc_b[js]])

        def v_chains(js):
            for tt in range(8):
                bank = 4 + tt % 2
                fns = [lambda e, dt=dt, tt=tt, bank=bank: e.matmul(
                    PS[bank][:, :], xbf_v[:, dt, tt * 128:(tt + 1) * 128],
                    wvT_v[:, dt, :], start=(dt == 0), stop=(dt == 15)) for dt in range(16)]
                p.op(PE, fns, reads=[wvT_b, xbf_t[tt]], writes=[psb[bank]])
                p.op(DV, lambda e, tt=tt, js=js, bank=bank: e.tensor_copy(
                    vtm_v[:, tt, js * 512:(js + 1) * 512], PS[bank][:, :]),
                    reads=[psb[bank]], writes=[vtm_t[tt]])

        def k_sl(h, half):
            return k2_v[:, half * 4:(half + 1) * 4, h // 4, h % 4, :]

        def totals_block():
            for pr in range(8):
                r = pr % 2
                ktm_v = ktm[r][:, :].rearrange("p (hg h4 e) -> p hg h4 e", hg=4, h4=4, e=128)
                dma_tr(ktm_v, k2_v[:, pr, :, :, :], reads=k2_h, writes=[ktm_b[r]])
                fns = []
                for h in range(H):
                    bank = 4 + h // 4
                    fns.append(lambda e, h=h, pr=pr, bank=bank, ktm_v=ktm_v: e.matmul(
                        PS[bank][:, (h % 4) * 128:(h % 4 + 1) * 128],
                        ktm_v[:, h // 4, h % 4, :],
                        vtm_v[:, pr, h * 128:(h + 1) * 128],
                        start=(h % 4 == 0), stop=(h % 4 == 3)))
                p.op(PE, fns, reads=[ktm_b[r]] + vtm_t,
                     writes=[psb[4], psb[5], psb[6], psb[7]])
                for h in range(H):
                    bank = 4 + h // 4
                    p.op(DV, lambda e, h=h, bank=bank: e.tensor_tensor(
                        state_f[:, h * 128:(h + 1) * 128],
                        PS[bank][:, (h % 4) * 128:(h % 4 + 1) * 128],
                        state_f[:, h * 128:(h + 1) * 128], op=ALU.add),
                        reads=[psb[bank], sf_h[h]], writes=[sf_h[h]])

        for blk in range(NB):
            for tp in range(4):
                r = 0
                rows = slice(blk * T + tp * 256, blk * T + (tp + 1) * 256)
                dma_pg(xrow[r][:, :].rearrange("p (a d) -> p a d", d=D),
                       xkv[rows, :].rearrange("(a p) d -> p a d", p=128),
                       writes=[xrow_b[r]])
                for ji in range(2):
                    tt = 2 * tp + ji
                    dma_tr(xbf_v[:, :, tt * 128:(tt + 1) * 128],
                           xrow[r][:, ji * D:(ji + 1) * D],
                           reads=[xrow_b[r]], writes=[xbf_t[tt]])

            proj_prep(W["wk"], 0, cache=wkT_hbm, cached=(blk > 0))
            proj_prep(W["wk"], 1, cache=wkT_hbm, cached=(blk > 0))
            for h in range(H):
                if h % 4 == 0:
                    v_prep(h // 4, cached=(blk > 0))
                proj_head(W["wk"], h)
                if h + 2 < H:
                    proj_prep(W["wk"], h + 2, cache=wkT_hbm, cached=(blk > 0))
                proj_evac(h)
                if h > 0:
                    fmap(h - 1, "k1", "k2", "bk1", "bk2", k_sl, k2_h[h - 1], 1.0)
                if h % 4 == 3:
                    v_chains(h // 4)
            fmap(H - 1, "k1", "k2", "bk1", "bk2", k_sl, k2_h[H - 1], 1.0)
            if blk < NB - 1:
                totals_block()

        sS1b.close()

        sQ = contextlib.ExitStack()
        q2s = sb(sQ, "q2s", H * T, BF16)
        q2_h = [gb(f"q2_{h}") for h in range(H)]
        q2_v = q2s[:, :].rearrange("p (pr h c) -> p pr h c", h=H, c=128)

        def q_sl(h, half):
            return q2_v[:, half * 4:(half + 1) * 4, h, :]
        proj_prep(W["wq"], 0)
        proj_prep(W["wq"], 1)
        for h in range(H):
            proj_head(W["wq"], h)
            if h + 2 < H:
                proj_prep(W["wq"], h + 2)
            proj_evac(h)
            if h > 0:
                fmap(h - 1, "q1", "q2", "bq1", "bq2", q_sl, q2_h[h - 1], SCALE)
        fmap(H - 1, "q1", "q2", "bq1", "bq2", q_sl, q2_h[H - 1], SCALE)

        if "dbg_k2" in dbg:
            dma_pg(dbg["dbg_k2"].rearrange("p (pr hg h4 c) -> p pr hg h4 c", hg=4, h4=4, c=128), k2_v, reads=k2_h)
        if "dbg_v" in dbg:
            dma_pg(dbg["dbg_v"].rearrange("p (t j) -> p t j", j=D), vtm_v, reads=vtm_t)
        if "dbg_q2" in dbg:
            dma_pg(dbg["dbg_q2"].rearrange("p (pr h c) -> p pr h c", h=H, c=128), q2_v, reads=q2_h)

        sSA.close()
        p.barrier()

        sS2 = contextlib.ExitStack()
        o_rms = sb(sS2, "o_rms", 8 * D, BF16, side="right")
        orms_pr = [gb(f"orms{i}") for i in range(8)]
        orms_v = o_rms[:, :].rearrange("p (t hd) -> p t hd", hd=D)

        if "dbg_tot" in dbg:
            dma_pg(dbg["dbg_tot"].rearrange("p (h e) -> p h e", e=128),
                   state_f[:, :].rearrange("p (h e) -> p h e", e=128), reads=sf_h)

        def passA(pr):
            r = pr % 2
            ktm_v = ktm[r][:, :].rearrange("p (hg h4 e) -> p hg h4 e", hg=4, h4=4, e=128)
            dma_tr(ktm_v, k2_v[:, pr, :, :, :], reads=k2_h, writes=[ktm_b[r]])
            mv2_ = msT2[r][:, :].rearrange("p (h c) -> p h c", c=128)
            for h in range(H):
                bank = h % 2
                p.op(PE, lambda e, h=h, pr=pr, bank=bank: e.matmul(
                    PS[bank][:, 0:128], k2_v[:, pr, h // 4, h % 4, :],
                    q2_v[:, pr, h, :], start=True, stop=True),
                    reads=[k2_h[h], q2_h[h]], writes=[psb[bank]])
                p.op(DV, lambda e, h=h, bank=bank, mv2_=mv2_: e.tensor_tensor(
                    mv2_[:, h, :], PS[bank][:, 0:128], maskT[:, :], op=ALU.mult),
                    reads=[psb[bank], gb("maskT")], writes=[msT_h[h]])

        def passB(pr):
            r = pr % 2
            ktm_v = ktm[r][:, :].rearrange("p (hg h4 e) -> p hg h4 e", hg=4, h4=4, e=128)
            mv2_ = msT2[r][:, :].rearrange("p (h c) -> p h c", c=128)
            for h in range(H):
                ob = [2, 3, 6, 7][h % 4]
                kb = 4 + h % 2
                p.op(AC, lambda e, h=h: e.copy(
                    sbr[h % 4][:, :], state_f[:, h * 128:(h + 1) * 128]),
                    reads=[sf_h[h]], writes=[sbf_h[h]])
                fns = [
                    lambda e, h=h, pr=pr, ob=ob, mv2_=mv2_: e.matmul(
                        PS[ob][:, 0:128], mv2_[:, h, :], vtm_v[:, pr, h * 128:(h + 1) * 128],
                        start=True, stop=False),
                    lambda e, h=h, pr=pr, ob=ob: e.matmul(
                        PS[ob][:, 0:128], q2_v[:, pr, h, :],
                        sbr[h % 4][:, :], start=False, stop=True),
                ]
                p.op(PE, fns, reads=[msT_h[h], vtm_t[pr], q2_h[h], sbf_h[h]], writes=[psb[ob]])
                p.op(PE, lambda e, h=h, pr=pr, kb=kb, ktm_v=ktm_v: e.matmul(
                    PS[kb][:, 0:128], ktm_v[:, h // 4, h % 4, :],
                    vtm_v[:, pr, h * 128:(h + 1) * 128],
                    start=True, stop=True),
                    reads=[ktm_b[r], vtm_t[pr]], writes=[psb[kb]])
                p.op(DV, lambda e, h=h, kb=kb: e.tensor_tensor(
                    state_f[:, h * 128:(h + 1) * 128], PS[kb][:, 0:128],
                    state_f[:, h * 128:(h + 1) * 128], op=ALU.add),
                    reads=[psb[kb], sf_h[h], sbf_h[h]], writes=[sf_h[h]])
                rr2 = h % 2
                p.op(AC, lambda e, ob=ob, rr2=rr2: e.activation(
                    osc2[rr2][:, :], PS[ob][:, 0:128], AF.Square, accum_out=ssq2[rr2][:, :]),
                    reads=[psb[ob]], writes=[s3s[rr2]["osc"], s3s[rr2]["ssq"]])
                p.op(AC, lambda e, rr2=rr2: e.activation(
                    srt2[rr2][:, :], ssq2[rr2][:, :], AF.Sqrt, bias=eps5[:, :], scale=1.0 / DK),
                    reads=[s3s[rr2]["ssq"], gb("eps5")], writes=[s3s[rr2]["srt"]])
                p.op(DV, lambda e, rr2=rr2: e.reciprocal(rst2[rr2][:, :], srt2[rr2][:, :]),
                     reads=[s3s[rr2]["srt"]], writes=[s3s[rr2]["rst"]])
                p.op(DV, lambda e, h=h, pr=pr, ob=ob, rr2=rr2: e.tensor_scalar(
                    orms_v[:, pr, h * 128:(h + 1) * 128], PS[ob][:, 0:128],
                    rst2[rr2][:, :], None, op0=ALU.mult),
                    reads=[psb[ob], s3s[rr2]["rst"]], writes=[orms_pr[pr]])

        passA(0)
        for pr in range(1, 8):
            passA(pr)
            passB(pr - 1)
        passB(7)
        if "dbg_orms" in dbg:
            dma_pg(dbg["dbg_orms"].rearrange("p (t j) -> p t j", j=D), orms_v, reads=orms_pr)

        sQ.close()
        sS1.close()
        sSR.close()
        p.barrier()

        sS2b = contextlib.ExitStack()
        o_fm = sb(sS2b, "o_fm", H * T, BF16, side="right")
        ofm_t = [gb(f"ofm{i}") for i in range(8)]
        ofm_v = o_fm[:, :].rearrange("p (h t) -> p h t", t=T)
        for pr in range(8):
            outap = ofm_v[:, :, pr * 128:(pr + 1) * 128]
            dma_tr(outap, orms_v[:, pr, :], reads=[orms_pr[pr]], writes=[ofm_t[pr]])
            p.op(DV, lambda e, outap=outap: e.tensor_scalar(
                outap, outap, bcol["nw"][:, :], None, op0=ALU.mult),
                reads=[ofm_t[pr], gb("c_nw")], writes=[ofm_t[pr]])

        sS3 = contextlib.ExitStack()
        u_sb = sb(sS3, "u_sb", 8 * D, F32)
        sS3a = contextlib.ExitStack()
        woT = [sb(sS3a, f"woT{i}", H * 512, BF16) for i in range(2)]
        xres = [sb(sS3a, f"xres{i}", 512, F32) for i in range(2)]
        woT_b = [gb("woT0"), gb("woT1")]
        xres_b = [gb("xres0"), gb("xres1")]
        u_t = [gb(f"u_t{i}") for i in range(8)]
        u_v = u_sb[:, :].rearrange("p (tt o) -> p tt o", o=D)
        stats5_v = stats5[:, :].rearrange("p (t o s) -> p t o s", o=4, s=6)
        st5_t = [gb(f"st5_{i}") for i in range(8)]
        def wo_prep(os_):
            so = os_ % 2
            woT_v = woT[so][:, :].rearrange("p (j o) -> p j o", o=512)
            for jp in range(2):
                r = jp % 2
                rows = slice(os_ * 512 + jp * 256, os_ * 512 + (jp + 1) * 256)
                dma_pg(wrow[r][:, :].rearrange("p (a d) -> p a d", d=D),
                       W["wo"][rows, :].rearrange("(a p) d -> p a d", p=128),
                       writes=[wrow_b[r]])
                for jj in range(2):
                    dma_tr(woT_v[:, :, (2 * jp + jj) * 128:(2 * jp + jj + 1) * 128],
                           wrow[r][:, jj * D:(jj + 1) * D],
                           reads=[wrow_b[r]], writes=[woT_b[so]])
            return woT_v

        wo_prep(0)
        wo_prep(1)
        for os_ in range(4):
            so = os_ % 2
            woT_v = woT[so][:, :].rearrange("p (j o) -> p j o", o=512)
            if os_ + 2 < 4:
                pass
            for tt in range(8):
                bank = tt % 4
                fns = [lambda e, jt=jt, tt=tt, bank=bank, woT_v=woT_v: e.matmul(
                    PS[bank][:, :], ofm_v[:, jt, tt * 128:(tt + 1) * 128],
                    woT_v[:, jt, :], start=(jt == 0), stop=(jt == 15)) for jt in range(16)]
                p.op(PE, fns, reads=[woT_b[so]] + ofm_t, writes=[psb[bank]])
                xr = tt % 2
                dma_pg(xres[xr][:, :],
                       xkv[3 * T + tt * 128:3 * T + (tt + 1) * 128, os_ * 512:(os_ + 1) * 512],
                       writes=[xres_b[xr]])
                p.op(DV, lambda e, bank=bank, xr=xr, tt=tt, os_=os_: e.tensor_tensor(
                    u_v[:, tt, os_ * 512:(os_ + 1) * 512], PS[bank][:, :], xres[xr][:, :],
                    op=ALU.add),
                    reads=[psb[bank], xres_b[xr]], writes=[u_t[tt]])
                p.op(DV, lambda e, tt=tt, os_=os_: e.bn_stats(
                    stats5_v[:, tt, os_, :], u_v[:, tt, os_ * 512:(os_ + 1) * 512]),
                    reads=[u_t[tt]], writes=[st5_t[tt]])
                if tt == 7 and os_ + 2 < 4:
                    wo_prep(os_ + 2)

        sS2b.close()
        sS2.close()
        sS3a.close()
        p.barrier()

        sS4a = contextlib.ExitStack()
        yr = [sb(sS4a, f"yr{i}", D, F32) for i in range(2)]
        yr_b = [gb("yr0"), gb("yr1")]
        zr = sb(sS4a, "zr", D, F32)
        tr2 = [sb(sS4a, f"tr2_{i}", 1024, F32) for i in range(2)]
        dr2 = [sb(sS4a, f"dr2_{i}", 1024, F32) for i in range(2)]
        i32r2 = [sb(sS4a, f"i32r2_{i}", 1024, I32) for i in range(2)]
        kfr2 = [sb(sS4a, f"kfr2_{i}", 1024, F32) for i in range(2)]
        xqr2 = [sb(sS4a, f"xqr2_{i}", 1024, BF16) for i in range(2)]
        tb2 = [gb("tr2_0"), gb("tr2_1")]
        db2 = [gb("dr2_0"), gb("dr2_1")]
        i32b2 = [gb("i32r2_0"), gb("i32r2_1")]
        kfb2 = [gb("kfr2_0"), gb("kfr2_1")]
        xqb2 = [gb("xqr2_0"), gb("xqr2_1")]
        rmaxc = sb(sS4a, "rmaxc", 1, F32)
        nmc = [sb(sS4a, f"nmc{i}", 1, F32) for i in range(2)]
        ysum = sb(sS4a, "ysum", 1, F32)
        ysq = sb(sS4a, "ysq", 1, F32)
        mvv = sb(sS4a, "mvv", 1, F32)
        oscr6 = sb(sS4a, "oscr6", D, F32)
        ysum_b, ysq_b, mvv_b, oscr6_b = gb("ysum"), gb("ysq"), gb("mvv"), gb("oscr6")
        rcc = sb(sS4a, "rcc", 1, F32)
        scc = sb(sS4a, "scc", 1, F32)
        rscc = sb(sS4a, "rscc", 1, F32)
        sS5 = contextlib.ExitStack()
        xq_fm = sb(sS5, "xq_fm", H * T, BF16, side="right")
        xq_t = [gb(f"xq{i}") for i in range(8)]
        xq_v = xq_fm[:, :].rearrange("p (d t) -> p d t", t=T)
        zb = gb("zr")
        nmb = [gb("nmc0"), gb("nmc1")]
        rmaxb, rccb, sccb, rscb = gb("rmaxc"), gb("rcc"), gb("scc"), gb("rscc")
        yhbm_t = [gb(f"yhbm{i}") for i in range(8)]

        for tt in range(8):
            r = tt % 2
            p.op(DV, lambda e, tt=tt: e.bn_aggr(
                mv[:, :], stats5_v[:, tt, :, :].rearrange("p o s -> p (o s)")),
                reads=[st5_t[tt]], writes=[scr["mv"]])
            p.op(AC, lambda e: e.activation(srt[:, :], mv[:, 1:2], AF.Sqrt, bias=eps5[:, :]),
                 reads=[scr["mv"], gb("eps5")], writes=[scr["srt"]])
            p.op(DV, lambda e: e.reciprocal(rstd[:, :], srt[:, :]),
                 reads=[scr["srt"]], writes=[scr["rstd"]])
            p.op(DV, lambda e: e.scalar_tensor_tensor(
                nmc[0][:, :], mv[:, 0:1], -1.0, rstd[:, :], op0=ALU.mult, op1=ALU.mult),
                reads=[scr["mv"], scr["rstd"]], writes=[nmb[0]])
            p.op(AC, lambda e, r=r, tt=tt: e.activation(
                yr[r][:, :], u_v[:, tt, :], AF.Identity, bias=nmc[0][:, :], scale=rstd[:, :],
                accum_out=ysum[:, :]),
                reads=[u_t[tt], nmb[0], scr["rstd"]], writes=[yr_b[r], ysum_b])
            dma_pg(y_hbm[tt * 128:(tt + 1) * 128, :], yr[r][:, :],
                   reads=[yr_b[r]], writes=[yhbm_t[tt]])
            for i in range(4):
                p.op(DV, lambda e, r=r, i=i: e.bn_stats(
                    bnst[:, i * 6:(i + 1) * 6], yr[r][:, i * 512:(i + 1) * 512]),
                    reads=[yr_b[r]], writes=[scr["bnst"]])
            p.op(DV, lambda e: e.bn_aggr(mv[:, :], bnst[:, :]),
                 reads=[scr["bnst"]], writes=[scr["mv"]])
            p.op(AC, lambda e: e.activation(srt[:, :], mv[:, 1:2], AF.Sqrt, bias=eps8[:, :]),
                 reads=[scr["mv"], gb("eps8")], writes=[scr["srt"]])
            p.op(DV, lambda e: e.reciprocal(rstd[:, :], srt[:, :]),
                 reads=[scr["srt"]], writes=[scr["rstd"]])
            p.op(DV, lambda e: e.scalar_tensor_tensor(
                nmc[1][:, :], mv[:, 0:1], -1.0, rstd[:, :], op0=ALU.mult, op1=ALU.mult),
                reads=[scr["mv"], scr["rstd"]], writes=[nmb[1]])
            p.op(AC, lambda e, r=r: e.activation(
                zr[:, :], yr[r][:, :], AF.Identity, bias=nmc[1][:, :], scale=rstd[:, :]),
                reads=[yr_b[r], nmb[1], scr["rstd"]], writes=[zb])
            p.op(DV, lambda e: e.reduce_max(rmaxc[:, :], zr[:, :], axis=AX.X,
                                            apply_absolute_value=True),
                 reads=[zb], writes=[rmaxb])
            p.op(DV, lambda e: e.reciprocal(rcc[:, :], rmaxc[:, :]), reads=[rmaxb], writes=[rccb])
            p.op(DV, lambda e: e.tensor_scalar(scc[:, :], rcc[:, :], 127.0, None, op0=ALU.mult),
                 reads=[rccb], writes=[sccb])
            p.op(DV, lambda e: e.tensor_scalar(rscc[:, :], rmaxc[:, :], 1.0 / 127.0, None,
                                               op0=ALU.mult),
                 reads=[rmaxb], writes=[rscb])
            for hf in range(2):
                sl = slice(hf * 1024, (hf + 1) * 1024)
                tbh, dbh, i32bh, kfbh, xqbh = tb2[hf], db2[hf], i32b2[hf], kfb2[hf], xqb2[hf]
                p.op(AC, lambda e, sl=sl, hf=hf: e.activation(
                    tr2[hf][:, :], zr[:, sl], AF.Identity, scale=scc[:, :]),
                    reads=[zb, sccb], writes=[tbh])
                p.op(DV, lambda e, hf=hf: e.tensor_copy(i32r2[hf][:, :], tr2[hf][:, :]),
                     reads=[tbh], writes=[i32bh])
                p.op(DV, lambda e, hf=hf: e.tensor_copy(kfr2[hf][:, :], i32r2[hf][:, :]),
                     reads=[i32bh], writes=[kfbh])
                p.op(DV, lambda e, hf=hf: e.tensor_tensor(
                    dr2[hf][:, :], tr2[hf][:, :], kfr2[hf][:, :], op=ALU.subtract),
                    reads=[tbh, kfbh], writes=[dbh])
                p.op(DV, lambda e, hf=hf: e.scalar_tensor_tensor(
                    tr2[hf][:, :], dr2[hf][:, :], 0.5, kfr2[hf][:, :],
                    op0=ALU.is_ge, op1=ALU.add),
                    reads=[dbh, kfbh], writes=[tbh])
                p.op(DV, lambda e, hf=hf: e.scalar_tensor_tensor(
                    kfr2[hf][:, :], dr2[hf][:, :], -0.5, tr2[hf][:, :],
                    op0=ALU.is_gt, op1=ALU.add),
                    reads=[dbh, tbh], writes=[kfbh])
                p.op(DV, lambda e, hf=hf: e.tensor_scalar(
                    dr2[hf][:, :], kfr2[hf][:, :], 1.0, 127.0, op0=ALU.subtract, op1=ALU.min),
                    reads=[kfbh], writes=[dbh])
                p.op(DV, lambda e, hf=hf: e.tensor_scalar(
                    xqr2[hf][:, :], dr2[hf][:, :], -128.0, rscc[:, :],
                    op0=ALU.max, op1=ALU.mult),
                    reads=[dbh, rscb], writes=[xqbh])
                dma_tr(xq_v[:, hf * 8:(hf + 1) * 8, tt * 128:(tt + 1) * 128],
                       xqr2[hf][:, :], reads=[xqbh], writes=[xq_t[tt]])

        if "dbg_xq" in dbg:
            dma_pg(dbg["dbg_xq"].rearrange("p (d t) -> p d t", t=T), xq_v, reads=xq_t)

        sS4a.close()
        sS3.close()
        p.barrier()

        sAcc = contextlib.ExitStack()
        acc = sb(sAcc, "acc", 8 * D, F32)
        sS6 = contextlib.ExitStack()
        ringg = [sb(sS6, f"ringg{i}", H * 128, BF16) for i in range(2)]
        ringu = [sb(sS6, f"ringu{i}", H * 128, BF16) for i in range(2)]
        gsb = [sb(sS6, "gsb0", T, F32)] * 2
        pgr = sb(sS6, "pgr", 8 * T, BF16)
        sgn = [sb(sS6, f"sgn{i}", 2 * D, BF16) for i in range(2)]
        wds = [sb(sS6, f"wds{i}", 2 * 1024, BF16) for i in range(2)]
        wdsg = [sb(sS6, f"wdsg{i}", 2 * 1024, BF16) for i in range(2)]
        wdT = [sb(sS6, f"wdT{i}", 8 * 512, BF16) for i in range(2)]
        rg_b = [gb("ringg0"), gb("ringg1")]
        ru_b = [gb("ringu0"), gb("ringu1")]
        gsb_b = [gb("gsb0"), gb("gsb0")]
        pgr_b = [gb(f"pgr{i}") for i in range(8)]
        sgn_b = [gb("sgn0"), gb("sgn1")]
        wds_b = [gb("wds0"), gb("wds1")]
        wdsg_b = [gb("wdsg0"), gb("wdsg1")]
        wdT_b = [gb("wdT0"), gb("wdT1")]
        acc_b = [gb(f"acc{i}") for i in range(8)]
        acc_v = acc[:, :].rearrange("p (tt o) -> p tt o", o=D)
        pgr_v = pgr[:, :].rearrange("p (rt t) -> p rt t", t=T)

        def ring_prep(rg_, rc):
            rr = rc % 2
            if rc % 2 == 0:
                rows = slice(rg_ * 1024 + rc * 128, rg_ * 1024 + (rc + 2) * 128)
                for (wname, wi, sgi) in (("wg", 0, 0), ("wu", 2, 1)):
                    dma_pg(wrow[wi][:, :].rearrange("p (a d) -> p a d", d=D),
                           W[wname][rows, :].rearrange("(a p) d -> p a d", p=128),
                           writes=[wrow_b[wi]])
                    p.op(AC, lambda e, wi=wi, sgi=sgi: e.sign(sgn[sgi][:, :], wrow[wi][:, :]),
                         reads=[wrow_b[wi]], writes=[sgn_b[sgi]])
            for (ring, ring_b, sgi) in ((ringg, rg_b, 0), (ringu, ru_b, 1)):
                rv = ring[rr][:, :].rearrange("p (d j) -> p d j", j=128)
                dma_tr(rv, sgn[sgi][:, (rc % 2) * D:(rc % 2 + 1) * D],
                       reads=[sgn_b[sgi]], writes=[ring_b[rr]])

        ring_prep(0, 0)
        ring_prep(0, 1)
        for rg_ in range(8):
            for rc in range(8):
                rr = rc % 2
                for half in range(2):
                    gbank, ubank = half, 2 + half
                    for (bank, ring, ring_b) in ((gbank, ringg, rg_b), (ubank, ringu, ru_b)):
                        rv = ring[rr][:, :].rearrange("p (d j) -> p d j", j=128)
                        fns = [lambda e, dt=dt, half=half, bank=bank, rv=rv: e.matmul(
                            PS[bank][:, :], rv[:, dt, :],
                            xq_v[:, dt, half * 512:(half + 1) * 512],
                            start=(dt == 0), stop=(dt == 15)) for dt in range(16)]
                        p.op(PE, fns, reads=[ring_b[rr]] + xq_t[half * 4:(half + 1) * 4],
                             writes=[psb[bank]])
                    p.op(AC, lambda e, half=half, rr=rr, gbank=gbank: e.activation(
                        gsb[rr][:, half * 512:(half + 1) * 512], PS[gbank][:, :], AF.Silu),
                        reads=[psb[gbank]], writes=[gsb_b[rr]])
                    p.op(DV, lambda e, half=half, rr=rr, rc=rc, ubank=ubank: e.tensor_tensor(
                        pgr_v[:, rc, half * 512:(half + 1) * 512], PS[ubank][:, :],
                        gsb[rr][:, half * 512:(half + 1) * 512], op=ALU.mult),
                        reads=[psb[ubank], gsb_b[rr]], writes=[pgr_b[rc]])
                nc_, ng_ = rc + 2, rg_
                if nc_ >= 8:
                    nc_, ng_ = nc_ - 8, rg_ + 1
                if ng_ < 8:
                    ring_prep(ng_, nc_)
            if rg_ == 0 and "dbg_p0" in dbg:
                dma_pg(dbg["dbg_p0"].rearrange("p (rt t) -> p rt t", t=T), pgr_v, reads=pgr_b)
            if "dbg_pall" in dbg:
                dma_pg(dbg["dbg_pall"][rg_].rearrange("p (rt t) -> p rt t", t=T), pgr_v, reads=pgr_b)
            for os_ in range(4):
                so = os_ % 2
                wdT_v = wdT[so][:, :].rearrange("p (rt o) -> p rt o", o=512)
                for op_ in range(2):
                    rr = op_ % 2
                    orow = os_ * 512 + op_ * 256
                    dma_pg(wds[rr][:, :].rearrange("p (a d) -> p a d", d=1024),
                           W["wd"][orow:orow + 256, rg_ * 1024:(rg_ + 1) * 1024]
                           .rearrange("(a p) d -> p a d", p=128),
                           writes=[wds_b[rr]])
                    p.op(AC, lambda e, rr=rr: e.sign(wdsg[rr][:, :], wds[rr][:, :]),
                         reads=[wds_b[rr]], writes=[wdsg_b[rr]])
                    for oo2 in range(2):
                        oo = 2 * op_ + oo2
                        dma_tr(wdT_v[:, :, oo * 128:(oo + 1) * 128],
                               wdsg[rr][:, oo2 * 1024:(oo2 + 1) * 1024],
                               reads=[wdsg_b[rr]], writes=[wdT_b[so]])
                if rg_ == 0 and os_ == 0 and "dbg_wdt" in dbg:
                    dma_pg(dbg["dbg_wdt"].rearrange("p (rt o) -> p rt o", o=512), wdT_v,
                           reads=[wdT_b[so]])
                for tt in range(8):
                    bank = 4 + (tt % 4)
                    fns = [lambda e, rt=rt, tt=tt, bank=bank, wdT_v=wdT_v: e.matmul(
                        PS[bank][:, :], pgr_v[:, rt, tt * 128:(tt + 1) * 128],
                        wdT_v[:, rt, :], start=(rt == 0), stop=(rt == 7)) for rt in range(8)]
                    p.op(PE, fns, reads=[wdT_b[so]] + pgr_b, writes=[psb[bank]])
                    if rg_ == 0:
                        p.op(DV, lambda e, bank=bank, tt=tt, os_=os_: e.tensor_copy(
                            acc_v[:, tt, os_ * 512:(os_ + 1) * 512], PS[bank][:, :]),
                            reads=[psb[bank]], writes=[acc_b[tt]])
                    else:
                        p.op(DV, lambda e, bank=bank, tt=tt, os_=os_: e.tensor_tensor(
                            acc_v[:, tt, os_ * 512:(os_ + 1) * 512], PS[bank][:, :],
                            acc_v[:, tt, os_ * 512:(os_ + 1) * 512], op=ALU.add),
                            reads=[psb[bank], acc_b[tt]], writes=[acc_b[tt]])

        if "dbg_p0" in dbg:
            pass
        if "dbg_acc" in dbg:
            dma_pg(dbg["dbg_acc"].rearrange("(tt p) o -> p tt o", p=128), acc_v, reads=acc_b)
        if "dbg_yh" in dbg:
            dma_pg(dbg["dbg_yh"], y_hbm, reads=yhbm_t)
        sS5.close()
        sS6.close()
        p.barrier()

        sS7 = contextlib.ExitStack()
        yr2 = [sb(sS7, f"yr2_{i}", D, F32) for i in range(2)]
        srt8 = [sb(sS7, f"srt8_{i}", 1, F32) for i in range(2)]
        rst8 = [sb(sS7, f"rst8_{i}", 1, F32) for i in range(2)]
        srt8_b = [gb("srt8_0"), gb("srt8_1")]
        rst8_b = [gb("rst8_0"), gb("rst8_1")]
        u2b = [sb(sS7, f"u2b{i}", D, F32) for i in range(2)]
        outr = [sb(sS7, f"outr{i}", D, F32) for i in range(2)]
        yr2_b = [gb("yr2_0"), gb("yr2_1")]
        u2b_b = [gb("u2b0"), gb("u2b1")]
        outr_b = [gb("outr0"), gb("outr1")]
        for tt in range(8):
            r = tt % 2
            dma_pg(yr2[r][:, :], y_hbm[tt * 128:(tt + 1) * 128, :],
                   reads=[yhbm_t[tt]], writes=[yr2_b[r]])
            p.op(DV, lambda e, r=r, tt=tt: e.tensor_tensor(
                u2b[r][:, :], acc_v[:, tt, :], yr2[r][:, :], op=ALU.add),
                reads=[acc_b[tt], yr2_b[r]], writes=[u2b_b[r]])
            bn_, mv_ = (bnst, mv) if r == 0 else (bnst2, mv2)
            bnb_, mvb_ = (gb("bnst"), gb("mv")) if r == 0 else (gb("bnst2"), gb("mv2"))
            srt_, rst_ = (srt8[r], rst8[r])
            srtb_, rstb_ = srt8_b[r], rst8_b[r]
            for i in range(4):
                p.op(DV, lambda e, r=r, i=i, bn_=bn_: e.bn_stats(
                    bn_[:, i * 6:(i + 1) * 6], u2b[r][:, i * 512:(i + 1) * 512]),
                    reads=[u2b_b[r]], writes=[bnb_])
            p.op(DV, lambda e, bn_=bn_, mv_=mv_: e.bn_aggr(mv_[:, :], bn_[:, :]),
                 reads=[bnb_], writes=[mvb_])
            p.op(AC, lambda e, mv_=mv_, srt_=srt_: e.activation(
                srt_[:, :], mv_[:, 1:2], AF.Sqrt, bias=eps5[:, :]),
                 reads=[mvb_, gb("eps5")], writes=[srtb_])
            p.op(DV, lambda e, srt_=srt_, rst_=rst_: e.reciprocal(rst_[:, :], srt_[:, :]),
                 reads=[srtb_], writes=[rstb_])
            p.op(DV, lambda e, r=r, mv_=mv_, rst_=rst_: e.tensor_scalar(
                outr[r][:, :], u2b[r][:, :], mv_[:, 0:1], rst_[:, :],
                op0=ALU.subtract, op1=ALU.mult),
                reads=[u2b_b[r], mvb_, rstb_], writes=[outr_b[r]])
            dma_pg(out[tt * 128:(tt + 1) * 128, :], outr[r][:, :], reads=[outr_b[r]])

        sS7.close()
        sAcc.close()

        p.raw_wait(PO, "dg", p.cnt["dg"])

        p.finalize()

        @block.tensor
        def _(e):
            for f in p.ops[PE]:
                f(e, S)

        @block.vector
        def _(e):
            for f in p.ops[DV]:
                f(e, S)

        @block.scalar
        def _(e):
            for f in p.ops[AC]:
                f(e, S)

        @block.gpsimd
        def _(e):
            for f in p.ops[PO]:
                f(e, S)

        @block.sync
        def _(e):
            for f in p.ops[SY]:
                f(e, S)

    return nc


_NC_CACHE = None


def kernel(L0_attn_x, L0_attn_wq, L0_attn_wk, L0_attn_wv, L0_attn_wo,
           L0_attn_fq1_w, L0_attn_fq1_b, L0_attn_fq2_w, L0_attn_fq2_b,
           L0_attn_fk1_w, L0_attn_fk1_b, L0_attn_fk2_w, L0_attn_fk2_b,
           L0_attn_norm_w, L0_ffn_W_g, L0_ffn_W_u, L0_ffn_W_d):
    x = np.ascontiguousarray(np.asarray(L0_attn_x, dtype=np.float32))
    ws = {n_: np.ascontiguousarray(np.asarray(a, dtype=np.float32)) for n_, a in [
        ("wq", L0_attn_wq), ("wk", L0_attn_wk), ("wv", L0_attn_wv), ("wo", L0_attn_wo),
        ("fq1", L0_attn_fq1_w), ("fq2", L0_attn_fq2_w),
        ("fk1", L0_attn_fk1_w), ("fk2", L0_attn_fk2_w),
        ("fq1b", L0_attn_fq1_b), ("fq2b", L0_attn_fq2_b),
        ("fk1b", L0_attn_fk1_b), ("fk2b", L0_attn_fk2_b),
        ("nw", L0_attn_norm_w),
        ("wg", L0_ffn_W_g), ("wu", L0_ffn_W_u), ("wd", L0_ffn_W_d)]}

    in_maps = []
    for c in range(NCORE):
        b, q = c // 4, c % 4
        n = T * (q + 1)
        xkv = np.zeros((NB * T, D), dtype=np.float32)
        xkv[NB * T - n:] = x[b, :n]
        m = {"xkv": xkv}
        m.update(ws)
        in_maps.append(m)

    global _NC_CACHE
    if _NC_CACHE is None:
        _NC_CACHE = build_nc()
    res = run_bass_kernel_spmd(_NC_CACHE, in_maps, core_ids=list(range(NCORE)))

    outa = np.empty((2, 4096, D), dtype=np.float32)
    for c in range(NCORE):
        b, q = c // 4, c % 4
        outa[b, q * T:(q + 1) * T] = res.results[c]["out"]
    if _DEBUG:
        kernel.debug = [res.results[c] for c in range(NCORE)]
    return outa
